# version 15 (final)
# speedup vs baseline: 1.1737x; 1.0174x over previous
import contextlib
import numpy as np
import concourse.bass as bass
import concourse.mybir as mybir
from concourse.bass_utils import run_bass_kernel_spmd

F32 = mybir.dt.float32
BF16 = mybir.dt.bfloat16
AF = mybir.ActivationFunctionType
ALU = mybir.AluOpType
AX = mybir.AxisListType

D = 1024
NCH = 8
EPS = 1e-5
ENGS = ("t", "v", "s", "g", "q")



class Em:
    def __init__(self, nc, stack):
        self.nc = nc
        self.fns = {e: [] for e in ENGS}
        self.cnt = {}
        self.sems = {}
        self.wm = {e: {} for e in ENGS}
        for n in list(ENGS) + ["dW", "dWb", "dW2", "dW2b", "dX", "dXb", "dO", "dS", "dSb", "dC", "dCb", "dQ", "dQb"]:
            self.sems[n] = stack.enter_context(nc.semaphore(n))
            self.cnt[n] = 0

    def on(self, e, fn, deps=(), inc=None, noinc=False):
        need = {}
        for dtok in deps:
            if dtok is None:
                continue
            sname, val = dtok
            if val > need.get(sname, 0):
                need[sname] = val
        for sname, val in need.items():
            if val <= self.wm[e].get(sname, 0):
                continue
            self.wm[e][sname] = val
            sem = self.sems[sname]
            self.fns[e].append(lambda eng, sem=sem, val=val: eng.wait_ge(sem, val))
        if noinc:
            self.fns[e].append(lambda eng, fn=fn: fn(eng))
            return None
        if inc is None:
            self.cnt[e] += 1
            tok, sem, amt = (e, self.cnt[e]), self.sems[e], 1
        else:
            sname, amt = inc
            self.cnt[sname] += amt
            tok, sem = (sname, self.cnt[sname]), self.sems[sname]
        self.fns[e].append(lambda eng, fn=fn, sem=sem, amt=amt: fn(eng).then_inc(sem, amt))
        return tok

    def dma(self, queue, sem_name, out, in_, deps=()):
        return self.on(queue, lambda eng: eng.dma_start(out=out, in_=in_),
                       deps=deps, inc=(sem_name, 16))

    def snap(self, engines="tvsg"):
        toks = [(e, self.cnt[e]) for e in engines if self.cnt[e] > 0]
        toks += [(n, self.cnt[n]) for n in ("dW", "dWb", "dW2", "dW2b", "dX", "dXb", "dO", "dS", "dSb") if self.cnt[n] > 0]
        return toks

    def finalize(self):
        with self.nc.Block() as block:
            @block.tensor
            def _(eng):
                for fn in self.fns["t"]:
                    fn(eng)

            @block.vector
            def _(eng):
                for fn in self.fns["v"]:
                    fn(eng)

            @block.scalar
            def _(eng):
                for fn in self.fns["s"]:
                    fn(eng)

            @block.gpsimd
            def _(eng):
                for fn in self.fns["g"]:
                    fn(eng)

            @block.sync
            def _(eng):
                for fn in self.fns["q"]:
                    fn(eng)


class Tb:

    def __init__(self, ap, init=()):
        self.ap = ap
        self.wtok = None
        self.readers = list(init)

    def wdeps(self):
        d = list(self.readers)
        if self.wtok is not None:
            d.append(self.wtok)
        return d

    def wrote(self, tok):
        self.wtok = tok
        self.readers = []

    def rdeps(self):
        return [self.wtok] if self.wtok is not None else []

    def read(self, tok):
        if tok is not None:
            self.readers.append(tok)


class Banks:
    def __init__(self, nc, stack):
        self.tens = [stack.enter_context(nc.psum_tensor(f"ps{i}", [128, 2048], F32))
                     for i in range(2)]
        self.wr = [None] * 8
        self.rd = [[] for _ in range(8)]

    def f32(self, i, n=512):
        return self.tens[i // 4][:, (i % 4) * 512:(i % 4) * 512 + n]

    def bf16(self, i):
        return self.tens[i // 4][:, (i % 4) * 512:(i % 4 + 1) * 512].bitcast(BF16)

    def use(self, i):
        d = list(self.rd[i])
        if self.wr[i] is not None:
            d.append(self.wr[i])
        return d

    def set_wr(self, i, tok):
        self.wr[i] = tok
        self.rd[i] = []

    def read(self, i, tok):
        if tok is not None:
            self.rd[i].append(tok)


class Arena:
    def __init__(self, nc, stack, kbytes=107):
        self.nbytes = kbytes * 1024
        self.t = stack.enter_context(nc.sbuf_tensor("arena", [128, self.nbytes // 2], BF16))
        self.off = 0
        self.marks = []

    def save(self):
        self.marks.append(self.off)

    def restore(self):
        self.off = self.marks.pop()

    def alloc(self, n_elem, dtype=BF16):
        esz = 2 if dtype == BF16 else 4
        self.off = (self.off + 31) & ~31
        start = self.off
        self.off += n_elem * esz
        assert self.off <= self.nbytes, f"arena overflow {self.off}>{self.nbytes}"
        v = self.t[:, start // 2:start // 2 + (n_elem * esz) // 2]
        return v.bitcast(dtype) if dtype != BF16 else v


MARK_NOPS = True


class Ctx:
    marks = None

    def mark(self, label):
        if self.marks is not None:
            self.marks.append((label, self.em.cnt["t"]))
        if MARK_NOPS:
            self.em.on("t", lambda eng: eng.nop(), noinc=True)

    def __init__(self, nc, em, ps, ar, S, b):
        self.nc, self.em, self.ps, self.ar, self.S, self.b = nc, em, ps, ar, S, b
        self.NT = S // 512



def ln_stats(ctx, src_tile, dep_src, with_mu_row=False, nch=NCH, dmodel=D, ones=None, prep=None):
    em, ps, b, S, NT = ctx.em, ctx.ps, ctx.b, ctx.S, ctx.NT
    mu = b["rows_f32"][0:1, 0:S]
    ex2 = b["rows_f32"][0:1, 2048:2048 + S]
    r_f = b["rows_f32"][0:1, 4096:4096 + S]
    r_bf = b["rows_bf"][0:1, 0:S]
    mr_bf = b["rows_bf"][0:1, 2048:2048 + S]
    mu_bf = b["rows_f32"][0:1, 4096:6144].bitcast(BF16)[0:1, 0:S]
    xsq = b["xsq"]
    ones_f = b["ones_col"] if ones is None else ones

    mu_toks = []
    sq_war = [None]
    prev_stats = []
    for t in range(NT):
        dep_t = list(dep_src)
        if prep is not None:
            dep_t = dep_t + prep(t, prev_stats)
        tok_sum = None
        for ci in range(nch):
            dep = (dep_t + ps.use(0)) if ci == 0 else ()
            src_ap = src_tile(ci, t)
            tok_sum = em.on("t", lambda eng, ci=ci, src_ap=src_ap, nch=nch: eng.matmul(
                ps.f32(0)[0:1, :], ones_f[:, 0:1], src_ap,
                start=(ci == 0), stop=(ci == nch - 1)),
                deps=dep, noinc=(ci != nch - 1))
        ps.set_wr(0, tok_sum)
        tok_sq = None
        for ci in range(nch):
            sdep = dep_t + ([sq_war[0]] if sq_war[0] else [])
            src_ap = src_tile(ci, t)
            tksq = em.on("s", lambda eng, src_ap=src_ap: eng.activation(
                out=xsq[:, 0:512], in_=src_ap,
                func=AF.Square), deps=sdep)
            mdep = [tksq] + (ps.use(1) if ci == 0 else [])
            tok_sq = em.on("t", lambda eng, ci=ci, nch=nch: eng.matmul(
                ps.f32(1)[0:1, :], b["ones_col"][:, 0:1], xsq[:, 0:512],
                start=(ci == 0), stop=(ci == nch - 1)), deps=mdep)
            sq_war[0] = tok_sq
        ps.set_wr(1, tok_sq)
        tk1 = em.on("v", lambda eng, t=t, dmodel=dmodel: eng.tensor_scalar_mul(
            mu[:, t * 512:(t + 1) * 512], ps.f32(0)[0:1, :], 1.0 / dmodel), deps=[tok_sum])
        tk2 = em.on("v", lambda eng, t=t, dmodel=dmodel: eng.tensor_scalar_mul(
            ex2[:, t * 512:(t + 1) * 512], ps.f32(1)[0:1, :], 1.0 / dmodel), deps=[tok_sq])
        ps.read(0, tk1)
        ps.read(1, tk2)
        mu_toks += [tk1, tk2]
        prev_stats = [tok_sum, tok_sq]

    t_msq = em.on("v", lambda eng: eng.tensor_mul(r_f, mu, mu), deps=mu_toks)
    t_var = em.on("v", lambda eng: eng.tensor_sub(ex2, ex2, r_f), deps=[t_msq])
    t_ln = em.on("s", lambda eng: eng.activation(out=r_f, in_=ex2, func=AF.Ln,
                                                 bias=EPS), deps=[t_var])
    t_r = em.on("s", lambda eng: eng.activation(out=ex2, in_=r_f, func=AF.Exp,
                                                scale=-0.5), deps=[t_ln])
    t_rbf = em.on("v", lambda eng: eng.tensor_scalar_mul(r_bf, ex2, 1.0), deps=[t_r])
    t_mrbf = em.on("v", lambda eng: eng.tensor_mul(mr_bf, mu, ex2), deps=[t_r])
    out = {"r_bf": r_bf, "mr_bf": mr_bf, "toks": [t_rbf, t_mrbf],
           "xdone": [tok_sum, ("s", em.cnt["s"])]}
    if with_mu_row:
        t_mubf = em.on("v", lambda eng: eng.tensor_scalar_mul(mu_bf, mu, 1.0),
                       deps=mu_toks + [t_mrbf])
        out["mu_bf"] = mu_bf
        out["toks"].append(t_mubf)
    return out


def ln_normalize_tile(ctx, stats, src_tile, dep_src, t, xn_dst, sc,
                      bank_r=0, bank_m=1, gamma_col=None, nch=NCH, tw=512):
    em, ps, b = ctx.em, ctx.ps, ctx.b
    ones_bf = b["ones_row_bf"]
    row2 = stats["mu_bf"] if gamma_col is not None else stats["mr_bf"]
    tk_r = em.on("t", lambda eng, t=t, tw=tw: eng.matmul(
        ps.f32(bank_r, tw), ones_bf, stats["r_bf"][0:1, t * tw:(t + 1) * tw],
        start=True, stop=True), deps=stats["toks"] + ps.use(bank_r))
    ps.set_wr(bank_r, tk_r)
    tk_m = em.on("t", lambda eng, t=t, tw=tw: eng.matmul(
        ps.f32(bank_m, tw), ones_bf, row2[0:1, t * tw:(t + 1) * tw],
        start=True, stop=True), deps=stats["toks"] + ps.use(bank_m))
    ps.set_wr(bank_m, tk_m)
    rB, mB = ps.f32(bank_r, tw), ps.f32(bank_m, tw)
    toks = []
    for ci in range(nch):
        scd = sc.wdeps()
        src_ap = src_tile(ci, t)
        xn_ap = xn_dst(ci)
        if gamma_col is None:
            t1 = em.on("v", lambda eng, src_ap=src_ap, tw=tw: eng.tensor_mul(
                sc.ap[:, 0:tw], src_ap, rB),
                deps=list(dep_src) + [tk_r] + scd)
            ps.read(bank_r, t1)
            t2 = em.on("v", lambda eng, xn_ap=xn_ap, tw=tw: eng.tensor_sub(
                xn_ap, sc.ap[:, 0:tw], mB), deps=[t1, tk_m])
            ps.read(bank_m, t2)
        else:
            t1 = em.on("v", lambda eng, src_ap=src_ap, tw=tw: eng.tensor_sub(
                sc.ap[:, 0:tw], src_ap, mB),
                deps=list(dep_src) + [tk_m] + scd)
            ps.read(bank_m, t1)
            t2 = em.on("v", lambda eng, xn_ap=xn_ap, ci=ci, tw=tw: eng.scalar_tensor_tensor(
                out=xn_ap, in0=sc.ap[:, 0:tw], scalar=gamma_col[:, ci:ci + 1],
                in1=rB, op0=ALU.mult, op1=ALU.mult), deps=[t1, tk_r])
            ps.read(bank_r, t2)
        sc.wrote(t1)
        sc.read(t2)
        toks.append(t2)
    return toks


def linear_groups(ctx, w_tile, rhs_tile, n_k, m_list, deps, banks, evac, n=512):
    em, ps = ctx.em, ctx.ps
    for j, m in enumerate(m_list):
        bi = banks[j % len(banks)]
        tok = None
        for k in range(n_k):
            dep = (list(deps) + ps.use(bi)) if k == 0 else ()
            tok = em.on("t", lambda eng, k=k, m=m, bi=bi, n=n: eng.matmul(
                ps.f32(bi, n), w_tile(k, m), rhs_tile(k),
                start=(k == 0), stop=(k == n_k - 1)),
                deps=dep, noinc=(k != n_k - 1))
        ps.set_wr(bi, tok)
        evac(m, bi, tok)



def sub_l0a(ctx, dram, xoff):
    em, ps, ar, b, S, NT = ctx.em, ctx.ps, ctx.ar, ctx.b, ctx.S, ctx.NT
    x = b["x_res"]
    ar.save()
    bar = em.snap()
    wbufs, wtoks, bcols = {}, [], {}
    for nm in ("wq", "wk", "wv", "wo"):
        wv = ar.alloc(8 * 1024)
        wbufs[nm] = wv
        wtoks.append(em.dma("g", "dW", out=wv.rearrange("p (c n) -> p c n", c=8),
                            in_=dram["l0a_" + nm].rearrange("(c p) n -> p c n", p=128),
                            deps=bar))
    for nm in ("bq", "bk", "bv", "bo"):
        bc = ar.alloc(8, F32)
        bcols[nm] = bc
        wtoks.append(em.dma("q", "dX", out=bc, in_=dram["l0a_" + nm], deps=bar))

    qb = ar.alloc(NCH * 512)
    kb = ar.alloc(NCH * 512)
    vb = ar.alloc(NCH * 512)
    xnb = ar.alloc(NCH * 512)
    ob = xnb
    spA = Tb(ar.alloc(512, F32))
    spB = Tb(ar.alloc(512, F32))
    sc = spA
    recd = Tb(ar.alloc(32, F32))
    state_sb = ar.alloc(8 * 66)
    state_f32 = ar.alloc(8 * 66, F32)
    vT = Tb(ar.alloc(80))
    kT = Tb(ar.alloc(64))
    aT = Tb(ar.alloc(128))
    oq = Tb(ar.alloc(64))

    t_ones = em.on("v", lambda eng: eng.memset(vT.ap[:, 64:65], 1.0), deps=bar)

    def xs(ci, t):
        return x[:, xoff + ci * S + t * 512: xoff + ci * S + (t + 1) * 512]

    stats = ln_stats(ctx, xs, dep_src=bar)

    SB = [6, 7]
    TRB, OQB, NUMB, SCB = 3, 5, 4, 2
    state_cp = [None] * 16
    band_snap = []
    for t in range(NT):
        xn_toks = ln_normalize_tile(
            ctx, stats, xs, dep_src=list(band_snap) + bar, t=t,
            xn_dst=lambda ci: xnb[:, ci * 512:(ci + 1) * 512], sc=sc)

        def proj(wname, bname, dst, softplus):
            wv, bc = wbufs[wname], bcols[bname]
            outs = []

            def evac(m, bi, mmtok):
                if softplus:
                    d1 = em.on("s", lambda eng, m=m, bi=bi: eng.activation(
                        out=spA.ap, in_=ps.f32(bi), func=AF.Abs,
                        bias=bc[:, m:m + 1]), deps=[mmtok] + spA.wdeps())
                    spA.wrote(d1)
                    d2 = em.on("s", lambda eng: eng.activation(
                        out=spB.ap, in_=spA.ap, func=AF.Exp, scale=-1.0),
                        deps=[d1] + spB.wdeps())
                    spA.read(d2)
                    spB.wrote(d2)
                    d3 = em.on("s", lambda eng: eng.activation(
                        out=spA.ap, in_=spB.ap, func=AF.Ln, bias=1.0),
                        deps=[d2] + spA.wdeps())
                    spB.read(d3)
                    spA.wrote(d3)
                    d4 = em.on("v", lambda eng, m=m, bi=bi: eng.tensor_scalar(
                        out=spB.ap, in0=ps.f32(bi), scalar1=bc[:, m:m + 1],
                        scalar2=0.0, op0=ALU.add, op1=ALU.max),
                        deps=[mmtok] + spB.wdeps())
                    spB.wrote(d4)
                    d5 = em.on("v", lambda eng, m=m: eng.tensor_add(
                        dst[:, m * 512:(m + 1) * 512], spA.ap, spB.ap),
                        deps=[d3, d4])
                    spA.read(d5)
                    spB.read(d5)
                    ps.read(bi, d1)
                    ps.read(bi, d4)
                    outs.append(d5)
                else:
                    d1 = em.on("s", lambda eng, m=m, bi=bi: eng.activation(
                        out=dst[:, m * 512:(m + 1) * 512], in_=ps.f32(bi),
                        func=AF.Identity, bias=bc[:, m:m + 1]), deps=[mmtok])
                    ps.read(bi, d1)
                    outs.append(d1)

            linear_groups(ctx, lambda k, m: wv[:, k * 1024 + m * 128:k * 1024 + (m + 1) * 128],
                          lambda k: xnb[:, k * 512:(k + 1) * 512],
                          NCH, list(range(NCH)),
                          deps=xn_toks + wtoks + band_snap, banks=[2, 4, 5], evac=evac)
            return outs

        qtoks = proj("wq", "bq", qb, True)
        ktoks = proj("wk", "bk", kb, True)
        vtoks = proj("wv", "bv", vb, False)
        pdeps = [qtoks[-1], ktoks[-1], vtoks[-1]]

        ob_toks = []
        for c in range(4):
            cg = t * 4 + c
            last_ot = None
            for h in range(16):
                bh = (h % 2) * 64
                cb = h // 2
                q_ap = qb[bh:bh + 64, cb * 512 + c * 128: cb * 512 + (c + 1) * 128]
                k_ap = kb[bh:bh + 64, cb * 512 + c * 128: cb * 512 + (c + 1) * 128]
                v_ap = vb[bh:bh + 64, cb * 512 + c * 128: cb * 512 + (c + 1) * 128]
                sb_i = (h // 2) % 2
                scol = (h // 4) * 66
                stP = ps.f32(SB[sb_i])[bh:bh + 64, scol:scol + 65]
                st_sb = state_sb[bh:bh + 64, sb_i * 264 + (h // 4) * 66:
                                 sb_i * 264 + (h // 4) * 66 + 65]

                t_sc = em.on("t", lambda eng, k_ap=k_ap, q_ap=q_ap: eng.matmul(
                    ps.f32(SCB)[:, 0:128], k_ap, q_ap, start=True, stop=True),
                    deps=pdeps + ps.use(SCB))
                ps.set_wr(SCB, t_sc)
                t_mask = em.on("v", lambda eng: eng.tensor_mul(
                    aT.ap[:, 0:128], ps.f32(SCB)[:, 0:128], b["triu_bf"]),
                    deps=[t_sc] + aT.wdeps())
                aT.wrote(t_mask)
                ps.read(SCB, t_mask)

                t_vt = em.on("t", lambda eng, v_ap=v_ap, bh=bh: eng.transpose(
                    ps.bf16(TRB)[:, 0:64], v_ap, b["ident"][bh:bh + 64, bh:bh + 64]),
                    deps=pdeps + ps.use(TRB))
                ps.set_wr(TRB, t_vt)
                t_vte = em.on("s", lambda eng: eng.copy(
                    out=vT.ap[:, 0:64], in_=ps.bf16(TRB)[:, 0:64]),
                    deps=[t_vt, t_ones] + vT.wdeps())
                vT.wrote(t_vte)
                ps.read(TRB, t_vte)

                t_kt = em.on("t", lambda eng, k_ap=k_ap, bh=bh: eng.transpose(
                    ps.bf16(TRB)[:, 64:128], k_ap, b["ident"][bh:bh + 64, bh:bh + 64]),
                    deps=pdeps + ps.use(TRB))
                ps.set_wr(TRB, t_kt)
                t_kte = em.on("s", lambda eng: eng.copy(
                    out=kT.ap[:, 0:64], in_=ps.bf16(TRB)[:, 64:128]),
                    deps=[t_kt] + kT.wdeps())
                kT.wrote(t_kte)
                ps.read(TRB, t_kte)

                dep_num = [t_mask, t_vte] + ps.use(NUMB)
                if cg > 0:
                    dep_num.append(state_cp[h])
                t_n = em.on("t", lambda eng, cg=cg: eng.matmul(
                    ps.f32(NUMB)[:, 0:65], aT.ap[:, 0:128], vT.ap[:, 0:65],
                    start=True, stop=(cg == 0)), deps=dep_num, noinc=(cg != 0))
                if cg > 0:
                    t_n = em.on("t", lambda eng, q_ap=q_ap, st_sb=st_sb: eng.matmul(
                        ps.f32(NUMB)[:, 0:65], q_ap, st_sb, start=False, stop=True),
                        deps=())
                aT.read(t_n)
                vT.read(t_n)
                ps.set_wr(NUMB, t_n)

                st_f = state_f32[bh:bh + 64, sb_i * 264 + (h // 4) * 66:
                                 sb_i * 264 + (h // 4) * 66 + 65]
                t_su = em.on("t", lambda eng, stP=stP: eng.matmul(
                    stP, kT.ap[:, 0:64], vT.ap[:, 0:65],
                    start=True, stop=True),
                    deps=[t_kte, t_vte] + ps.use(SB[sb_i]))
                kT.read(t_su)
                vT.read(t_su)
                ps.set_wr(SB[sb_i], t_su)
                if cg == 0:
                    t_acc = em.on("v", lambda eng, stP=stP, st_f=st_f: eng.tensor_copy(
                        out=st_f, in_=stP), deps=[t_su])
                else:
                    t_acc = em.on("v", lambda eng, stP=stP, st_f=st_f: eng.tensor_add(
                        st_f, st_f, stP), deps=[t_su, state_cp[h]])
                ps.read(SB[sb_i], t_acc)
                t_cp = em.on("v", lambda eng, st_f=st_f, st_sb=st_sb: eng.tensor_copy(
                    out=st_sb, in_=st_f), deps=[t_acc])
                state_cp[h] = t_cp

                t_rc = em.on("v", lambda eng: eng.reciprocal(
                    recd.ap[:, 0:1], ps.f32(NUMB)[:, 64:65]),
                    deps=[t_n] + recd.wdeps())
                recd.wrote(t_rc)
                ps.read(NUMB, t_rc)
                t_dv = em.on("v", lambda eng: eng.tensor_scalar(
                    out=oq.ap[:, 0:64], in0=ps.f32(NUMB)[:, 0:64],
                    scalar1=recd.ap[:, 0:1], scalar2=None, op0=ALU.mult),
                    deps=[t_rc] + oq.wdeps())
                oq.wrote(t_dv)
                recd.read(t_dv)
                ps.read(NUMB, t_dv)

                t_ot = em.on("t", lambda eng, bh=bh, cb=cb: eng.transpose(
                    ps.bf16(OQB)[bh:bh + 64, cb * 128:(cb + 1) * 128],
                    oq.ap[:, 0:64], b["ident"]),
                    deps=[t_dv] + (ps.use(OQB) if h == 0 else []))
                oq.read(t_ot)
                ps.set_wr(OQB, t_ot)
                last_ot = t_ot

            t_oe = em.on("v", lambda eng, c=c: eng.tensor_copy(
                out=ob.rearrange("p (cc tt) -> p cc tt", cc=8)[:, :, c * 128:(c + 1) * 128],
                in_=ps.bf16(OQB).rearrange("p (cc tt) -> p cc tt", cc=8)),
                deps=[last_ot])
            ps.read(OQB, t_oe)
            ob_toks.append(t_oe)

        def evac_o(m, bi, mmtok):
            x_ap = xs(m, t)
            tk = em.on("v", lambda eng, x_ap=x_ap, m=m, bi=bi: eng.scalar_tensor_tensor(
                out=x_ap, in0=ps.f32(bi), scalar=bcols["bo"][:, m:m + 1],
                in1=x_ap, op0=ALU.add, op1=ALU.add),
                deps=[mmtok] + xn_toks + stats["xdone"])
            ps.read(bi, tk)

        linear_groups(ctx, lambda k, m: wbufs["wo"][:, k * 1024 + m * 128:k * 1024 + (m + 1) * 128],
                      lambda k: ob[:, k * 512:(k + 1) * 512],
                      NCH, list(range(NCH)), deps=ob_toks, banks=[2, 4], evac=evac_o)
        band_snap = em.snap()

    ar.restore()


SUB_BUILDERS = {}
SUB_BUILDERS["l0a"] = sub_l0a

def emit_rope(ctx, buf_ap, psum_bank, cmap, smap, scrA, scrB, deps, map_deps):
    em, ps, b = ctx.em, ctx.ps, ctx.b
    n = buf_ap.shape[1]
    t_p = em.on("t", lambda eng: eng.matmul(
        ps.f32(psum_bank)[:, 0:n], b["rope_swap"], buf_ap, start=True, stop=True),
        deps=list(deps) + ps.use(psum_bank))
    ps.set_wr(psum_bank, t_p)
    t_u = em.on("v", lambda eng: eng.tensor_mul(
        scrA.ap[:, 0:n], ps.f32(psum_bank)[:, 0:n], smap[:, 0:n]),
        deps=[t_p] + list(map_deps) + scrA.wdeps())
    scrA.wrote(t_u)
    ps.read(psum_bank, t_u)
    t_w = em.on("v", lambda eng: eng.tensor_mul(
        scrB.ap[:, 0:n], buf_ap, cmap[:, 0:n]),
        deps=list(deps) + list(map_deps) + scrB.wdeps())
    scrB.wrote(t_w)
    t_r = em.on("v", lambda eng: eng.tensor_add(buf_ap, scrA.ap[:, 0:n], scrB.ap[:, 0:n]),
                deps=[t_u, t_w])
    scrA.read(t_r)
    scrB.read(t_r)
    return t_r


def sub_l0b(ctx, dram, xoff):
    em, ps, ar, b, S, NT = ctx.em, ctx.ps, ctx.ar, ctx.b, ctx.S, ctx.NT
    x = b["x_res"]
    ar.save()
    bar = em.snap()
    wq = ar.alloc(8 * 1024)
    wdkv = ar.alloc(8 * 256)
    wuk = ar.alloc(2 * 1024)
    wuv = ar.alloc(2 * 1024)
    wo = wq
    wq_last = [None]
    gcol = ar.alloc(2, F32)
    bcol = ar.alloc(2)
    wtoks = [
        em.dma("g", "dW", out=wq.rearrange("p (c n) -> p c n", c=8),
               in_=dram["l0b_wq"].rearrange("(c p) n -> p c n", p=128), deps=bar),
        em.dma("g", "dW", out=wdkv.rearrange("p (c n) -> p c n", c=8),
               in_=dram["l0b_wdkv"].rearrange("(c p) n -> p c n", p=128), deps=bar),
        em.dma("g", "dW", out=wuk.rearrange("p (c n) -> p c n", c=2),
               in_=dram["l0b_wuk"].rearrange("(c p) n -> p c n", p=128), deps=bar),
        em.dma("g", "dW", out=wuv.rearrange("p (c n) -> p c n", c=2),
               in_=dram["l0b_wuv"].rearrange("(c p) n -> p c n", p=128), deps=bar),
        em.dma("q", "dX", out=gcol, in_=dram["l0b_lng"], deps=bar),
        em.dma("g", "dW", out=bcol, in_=dram["l0b_lnb"], deps=bar),
    ]
    latf = ar.alloc(2 * S)
    latn = ar.alloc(2 * S)
    xn_t = ar.alloc(8 * 512)
    qp = ar.alloc(S)
    kp = ar.alloc(S)
    vp = ar.alloc(S)
    vT = ar.alloc(16 * 64)
    outp = ar.alloc(S)
    ckv = ar.alloc(16, F32)
    cmap_t = Tb(ar.alloc(512, F32))
    smap_t = Tb(ar.alloc(512, F32))
    spA = Tb(ar.alloc(512, F32))
    spB = Tb(ar.alloc(512, F32))
    A2b = [Tb(ar.alloc(2048)), Tb(ar.alloc(2048))]
    T2b = [Tb(ar.alloc(2048)), Tb(ar.alloc(2048))]
    N2b = [Tb(ar.alloc(16, F32)), Tb(ar.alloc(16, F32))]
    O2b = [Tb(ar.alloc(64)), Tb(ar.alloc(64))]
    sc = spA
    vT_tb = Tb(vT)
    qp_tb, kp_tb, vp_tb = Tb(qp), Tb(kp), Tb(vp)
    outp_tb = Tb(outp)

    def xs(ci, t):
        return x[:, xoff + ci * S + t * 512: xoff + ci * S + (t + 1) * 512]

    stats = ln_stats(ctx, xs, dep_src=bar)

    lat_toks = []
    for t in range(NT):
        xn_toks = ln_normalize_tile(
            ctx, stats, xs, dep_src=bar + lat_toks, t=t,
            xn_dst=lambda ci: xn_t[:, ci * 512:(ci + 1) * 512], sc=sc)
        t_sp = em.dma("q", "dS", out=dram["scr_xn"][:, t * 4096:(t + 1) * 4096],
                      in_=xn_t[:, :], deps=xn_toks)
        lat_toks.append(t_sp)

        def evac_lat(m, bi, mmtok, t=t):
            tk = em.on("s", lambda eng, m=m, bi=bi, t=t: eng.activation(
                out=latf[:, m * S + t * 512:m * S + (t + 1) * 512], in_=ps.f32(bi),
                func=AF.Identity), deps=[mmtok])
            ps.read(bi, tk)
            lat_toks.append(tk)
        linear_groups(ctx, lambda k, m: wdkv[:, k * 256 + m * 128:k * 256 + (m + 1) * 128],
                      lambda k: xn_t[:, k * 512:(k + 1) * 512],
                      NCH, [0, 1], deps=xn_toks + wtoks, banks=[2, 4], evac=evac_lat)

    lstats = ln_stats(ctx, lambda ci, t: latf[:, ci * S + t * 512:ci * S + (t + 1) * 512],
                      dep_src=lat_toks, with_mu_row=True, nch=2, dmodel=256,
                      ones=b["ones_col_bf"])
    latn_toks = []
    for t in range(NT):
        latn_toks += ln_normalize_tile(
            ctx, lstats,
            lambda ci, tt: latf[:, ci * S + tt * 512:ci * S + (tt + 1) * 512],
            dep_src=lat_toks, t=t,
            xn_dst=lambda ci, t=t: latn[:, ci * S + t * 512:ci * S + (t + 1) * 512],
            sc=spB, gamma_col=gcol, nch=2)

    t_ck = None
    for j, wv in enumerate((wuk, wuv)):
        for m in range(8):
            for kc in range(2):
                dep = (wtoks + ps.use(6)) if (j == 0 and m == 0 and kc == 0) else ()
                t_ck = em.on("t", lambda eng, j=j, m=m, kc=kc, wv=wv: eng.matmul(
                    ps.f32(6)[:, j * 8 + m:j * 8 + m + 1],
                    wv[:, kc * 1024 + m * 128:kc * 1024 + (m + 1) * 128],
                    bcol[:, kc:kc + 1].bitcast(F32) if False else bcol[:, kc:kc + 1],
                    start=(kc == 0), stop=(kc == 1)),
                    deps=dep, noinc=not (j == 1 and m == 7 and kc == 1))
    ps.set_wr(6, t_ck)
    t_ckv = em.on("v", lambda eng: eng.tensor_copy(out=ckv[:, :], in_=ps.f32(6)[:, 0:16]),
                  deps=[t_ck])
    ps.read(6, t_ckv)

    SC_QUAD = 0
    RPB, TRB, AVB, OTB = 4, 5, 6, 7
    out_sp = []
    for hp in range(8):
        q_toks = []
        for t in range(NT):
            t_xn = em.dma("q", "dX", out=xn_t[:, :],
                          in_=dram["scr_xn"][:, t * 4096:(t + 1) * 4096],
                          deps=em.snap("tv") + lat_toks)
            t_cm = em.dma("q", "dX", out=cmap_t.ap[:, :],
                          in_=dram["l0b_cmap"][:, t * 512:(t + 1) * 512],
                          deps=cmap_t.wdeps())
            t_sm = em.dma("q", "dX", out=smap_t.ap[:, :],
                          in_=dram["l0b_smap"][:, t * 512:(t + 1) * 512],
                          deps=smap_t.wdeps())
            cmap_t.wrote(t_cm)
            smap_t.wrote(t_sm)
            t_grp = ("dX", em.cnt["dX"])
            tok = None
            dep0 = wtoks + [t_grp] + qp_tb.wdeps() + ps.use(2)
            for k in range(NCH):
                tok = em.on("t", lambda eng, k=k, hp=hp: eng.matmul(
                    ps.f32(2), wq[:, k * 1024 + hp * 128:k * 1024 + (hp + 1) * 128],
                    xn_t[:, k * 512:(k + 1) * 512],
                    start=(k == 0), stop=(k == NCH - 1)),
                    deps=dep0 if k == 0 else (), noinc=(k != NCH - 1))
            ps.set_wr(2, tok)
            wq_last[0] = tok
            t_e = em.on("s", lambda eng, t=t: eng.copy(
                out=qp[:, t * 512:(t + 1) * 512], in_=ps.f32(2)), deps=[tok])
            ps.read(2, t_e)
            qp_tb.wrote(t_e)
            t_r = emit_rope(ctx, qp[:, t * 512:(t + 1) * 512], RPB,
                            cmap_t.ap, smap_t.ap, spA, spB,
                            deps=[t_e], map_deps=[t_grp])
            qp_tb.wrote(t_r)
            cmap_t.read(t_r)
            smap_t.read(t_r)
            q_toks.append(t_r)
            tok = None
            dep0 = wtoks + latn_toks + kp_tb.wdeps() + ps.use(2) + [t_r]
            for k in range(2):
                tok = em.on("t", lambda eng, k=k, hp=hp, t=t: eng.matmul(
                    ps.f32(2), wuk[:, k * 1024 + hp * 128:k * 1024 + (hp + 1) * 128],
                    latn[:, k * S + t * 512:k * S + (t + 1) * 512],
                    start=(k == 0), stop=(k == 1)),
                    deps=dep0 if k == 0 else (), noinc=(k != 1))
            ps.set_wr(2, tok)
            t_e = em.on("s", lambda eng, t=t, hp=hp: eng.activation(
                out=kp[:, t * 512:(t + 1) * 512], in_=ps.f32(2), func=AF.Identity,
                bias=ckv[:, hp:hp + 1]), deps=[tok, t_ckv])
            ps.read(2, t_e)
            kp_tb.wrote(t_e)
            t_rk = emit_rope(ctx, kp[:, t * 512:(t + 1) * 512], RPB,
                             cmap_t.ap, smap_t.ap, spA, spB,
                             deps=[t_e], map_deps=[t_grp])
            kp_tb.wrote(t_rk)
            cmap_t.read(t_rk)
            smap_t.read(t_rk)
            q_toks.append(t_rk)
            tok = None
            dep0 = wtoks + latn_toks + vp_tb.wdeps() + ps.use(2) + [t_rk]
            for k in range(2):
                tok = em.on("t", lambda eng, k=k, hp=hp, t=t: eng.matmul(
                    ps.f32(2), wuv[:, k * 1024 + hp * 128:k * 1024 + (hp + 1) * 128],
                    latn[:, k * S + t * 512:k * S + (t + 1) * 512],
                    start=(k == 0), stop=(k == 1)),
                    deps=dep0 if k == 0 else (), noinc=(k != 1))
            ps.set_wr(2, tok)
            t_e = em.on("s", lambda eng, t=t, hp=hp: eng.activation(
                out=vp[:, t * 512:(t + 1) * 512], in_=ps.f32(2), func=AF.Identity,
                bias=ckv[:, 8 + hp:8 + hp + 1]), deps=[tok, t_ckv])
            ps.read(2, t_e)
            vp_tb.wrote(t_e)
            q_toks.append(t_e)

        for hh in range(2):
            bh = hh * 64
            vt_toks = []
            nkc = S // 128
            for kc in range(nkc):
                t_vt = em.on("t", lambda eng, kc=kc, bh=bh: eng.transpose(
                    ps.bf16(TRB)[:, (kc % 8) * 64:(kc % 8) * 64 + 64],
                    vp[bh:bh + 64, kc * 128:(kc + 1) * 128],
                    b["ident"][bh:bh + 64, bh:bh + 64]),
                    deps=q_toks + (ps.use(TRB) if kc % 8 == 0 else []))
                ps.set_wr(TRB, t_vt)
                if kc % 8 == 7 or kc == nkc - 1:
                    lo = (kc // 8) * 8
                    t_ve = em.on("v", lambda eng, lo=lo, kc=kc: eng.tensor_copy(
                        out=vT[:, lo * 64:(kc + 1) * 64],
                        in_=ps.bf16(TRB)[:, 0:(kc - lo + 1) * 64]),
                        deps=[t_vt] + (vT_tb.wdeps() if lo == 0 else []))
                    ps.read(TRB, t_ve)
                    vt_toks.append(t_ve)
            vT_tb.wrote(vt_toks[-1])
            pend2 = {}

            def lb_s1a(qt, bh=bh, q_toks=q_toks, pend2=pend2):
                nk = (qt + 1) * 128
                tok = None
                for j0 in range(0, nk, 512):
                    w = min(512, nk - j0)
                    dep = (q_toks + ps.use(0) + ps.use(1) + ps.use(2) + ps.use(3)) if j0 == 0 else ()
                    tok = em.on("t", lambda eng, j0=j0, w=w, qt=qt, bh=bh: eng.matmul(
                        ps.tens[0][:, j0:j0 + w],
                        qp[bh:bh + 64, qt * 128:(qt + 1) * 128],
                        kp[bh:bh + 64, j0:j0 + w], start=True, stop=True),
                        deps=dep, noinc=(j0 + 512 < nk))
                for i4 in range(4):
                    ps.set_wr(i4, tok)
                pend2[qt] = [tok]

            def lb_s1b(qt, pend2=pend2):
                p = qt % 2
                nk = (qt + 1) * 128
                nx, ab = N2b[p], A2b[p]
                tok = pend2[qt][0]
                t_mk = em.on("v", lambda eng, qt=qt, nk=nk: eng.tensor_add(
                    ps.tens[0][:, nk - 128:nk], ps.tens[0][:, nk - 128:nk],
                    b["causal_add"]), deps=[tok])
                for i4 in range(4):
                    ps.set_wr(i4, t_mk)
                t_mx = em.on("v", lambda eng, nk=nk, nx=nx: eng.reduce_max(
                    nx.ap[:, 0:1], ps.tens[0][:, 0:nk], axis=AX.X, negate=True),
                    deps=[t_mk] + nx.wdeps())
                t_m8 = em.on("v", lambda eng, nx=nx: eng.tensor_scalar_mul(
                    nx.ap[:, 1:2], nx.ap[:, 0:1], 0.125), deps=[t_mx])
                nx.wrote(t_m8)
                t_ex = em.on("s", lambda eng, nk=nk, nx=nx, ab=ab: eng.activation(
                    out=ab.ap[:, 0:nk], in_=ps.tens[0][:, 0:nk], func=AF.Exp,
                    scale=0.125, bias=nx.ap[:, 1:2], accum_out=nx.ap[:, 2:3]),
                    deps=[t_m8] + ab.wdeps())
                ab.wrote(t_ex)
                nx.wrote(t_ex)
                for i4 in range(4):
                    ps.read(i4, t_ex)
                pend2[qt] = [t_ex]

            def lb_s2a(qt, pend2=pend2):
                p = qt % 2
                ab = A2b[p]
                t_ex = pend2[qt][0]
                tr5 = tr4 = None
                for kc in range(qt + 1):
                    tb = 5 if kc < 8 else 4
                    dtr = [t_ex] + (ps.use(5) + ps.use(4) if kc == 0 else [])
                    t_at = em.on("t", lambda eng, kc=kc, tb=tb, ab=ab: eng.transpose(
                        ps.bf16(tb)[:, (kc % 8) * 128:(kc % 8) * 128 + 128],
                        ab.ap[:, kc * 128:(kc + 1) * 128], b["ident"]),
                        deps=dtr)
                    ab.read(t_at)
                    if tb == 5:
                        tr5 = t_at
                    else:
                        tr4 = t_at
                ps.set_wr(5, tr5)
                if tr4 is not None:
                    ps.set_wr(4, tr4)
                pend2[qt] += [tr5, tr4]

            def lb_s2b(qt, vt_toks=vt_toks, pend2=pend2):
                p = qt % 2
                nx, ab, at, oq = N2b[p], A2b[p], T2b[p], O2b[p]
                t_ex, tr5, tr4 = pend2[qt][:3]
                n5 = min(qt + 1, 8)
                at_toks = []
                t_ae = em.on("v", lambda eng, n5=n5, at=at: eng.tensor_copy(
                    out=at.ap[:, 0:n5 * 128],
                    in_=ps.bf16(5)[:, 0:n5 * 128]),
                    deps=[tr5] + at.wdeps())
                ps.read(5, t_ae)
                at_toks.append(t_ae)
                if tr4 is not None:
                    n4 = qt + 1 - 8
                    t_ae = em.on("v", lambda eng, n4=n4, at=at: eng.tensor_copy(
                        out=at.ap[:, 1024:1024 + n4 * 128],
                        in_=ps.bf16(4)[:, 0:n4 * 128]),
                        deps=[tr4, at_toks[0]])
                    ps.read(4, t_ae)
                    at_toks.append(t_ae)
                at.wrote(at_toks[-1])
                tok = None
                for kc in range(qt + 1):
                    dep = (at_toks + vt_toks + ps.use(AVB)) if kc == 0 else ()
                    tok = em.on("t", lambda eng, kc=kc, qt=qt, at=at: eng.matmul(
                        ps.f32(AVB)[:, 0:64], at.ap[:, kc * 128:(kc + 1) * 128],
                        vT[:, kc * 64:(kc + 1) * 64],
                        start=(kc == 0), stop=(kc == qt)),
                        deps=dep, noinc=(kc != qt))
                ps.set_wr(AVB, tok)
                at.read(tok)
                vT_tb.read(tok)
                t_rc = em.on("v", lambda eng, nx=nx: eng.reciprocal(
                    nx.ap[:, 3:4], nx.ap[:, 2:3]), deps=[t_ex])
                nx.wrote(t_rc)
                t_dv = em.on("v", lambda eng, nx=nx, oq=oq: eng.tensor_scalar(
                    out=oq.ap[:, 0:64], in0=ps.f32(AVB)[:, 0:64],
                    scalar1=nx.ap[:, 3:4], scalar2=None, op0=ALU.mult),
                    deps=[tok, t_rc] + oq.wdeps())
                oq.wrote(t_dv)
                nx.read(t_dv)
                ps.read(AVB, t_dv)
                pend2[qt].append(t_dv)

            def lb_s3(qt, bh=bh, hh=hh, pend2=pend2):
                p = qt % 2
                oq = O2b[p]
                t_dv = pend2[qt][-1]
                t_ot = em.on("t", lambda eng, qt=qt, bh=bh, oq=oq: eng.transpose(
                    ps.bf16(OTB)[bh:bh + 64, (qt % 4) * 128:(qt % 4) * 128 + 128],
                    oq.ap[:, 0:64], b["ident"]),
                    deps=[t_dv] + (ps.use(OTB) if qt % 4 == 0 else []))
                oq.read(t_ot)
                ps.set_wr(OTB, t_ot)
                if qt % 4 == 3:
                    t_oe = em.on("v", lambda eng, qt=qt, bh=bh: eng.tensor_copy(
                        out=outp[bh:bh + 64, (qt - 3) * 128:(qt + 1) * 128],
                        in_=ps.bf16(OTB)[bh:bh + 64, 0:512]),
                        deps=[t_ot] + (outp_tb.wdeps() if qt == 3 and hh == 0 else []))
                    ps.read(OTB, t_oe)
                    outp_tb.wrote(t_oe)

            NQ = S // 128
            for i3 in range(NQ + 2):
                if i3 < NQ:
                    lb_s1a(i3)
                if 1 <= i3 <= NQ:
                    lb_s2a(i3 - 1)
                if i3 < NQ:
                    lb_s1b(i3)
                if 1 <= i3 <= NQ:
                    lb_s2b(i3 - 1)
                if i3 >= 2:
                    lb_s3(i3 - 2)
        t_osp = em.dma("q", "dS", out=dram["scr_out"][:, hp * S:(hp + 1) * S],
                       in_=outp[:, :], deps=[outp_tb.wtok])
        outp_tb.read(t_osp)
        out_sp.append(t_osp)

    em.dma("g", "dW", out=wo.rearrange("p (c n) -> p c n", c=8),
           in_=dram["l0b_wo"].rearrange("(c p) n -> p c n", p=128),
           deps=[wq_last[0]])
    wtok_wo = [("dW", em.cnt["dW"])]
    for t in range(NT):
        t_ld = em.dma("q", "dX", out=xn_t[:, :],
                      in_=dram["scr_out"].rearrange("p (c tt) -> p c tt", c=8)[:, :, t * 512:(t + 1) * 512],
                      deps=out_sp + em.snap("t"))

        def evac_o(m, bi, mmtok, t=t):
            x_ap = xs(m, t)
            tk = em.on("v", lambda eng, x_ap=x_ap, bi=bi: eng.tensor_add(
                x_ap, ps.f32(bi), x_ap), deps=[mmtok] + stats["xdone"])
            ps.read(bi, tk)
        linear_groups(ctx, lambda k, m: wo[:, k * 1024 + m * 128:k * 1024 + (m + 1) * 128],
                      lambda k: xn_t[:, k * 512:(k + 1) * 512],
                      NCH, list(range(NCH)), deps=[t_ld] + wtok_wo, banks=[2, 4],
                      evac=evac_o)
    ar.restore()


SUB_BUILDERS["l0b"] = sub_l0b

def sub_l1a(ctx, dram, xoff):
    em, ps, ar, b, S, NT = ctx.em, ctx.ps, ctx.ar, ctx.b, ctx.S, ctx.NT
    x = b["x_res"]
    ar.save()
    bar = em.snap()
    BW = 256
    NB = S // BW
    wq = ar.alloc(8 * 1024)
    wk = ar.alloc(8 * 1024)
    wv = ar.alloc(8 * 1024)
    wo = ar.alloc(8 * 1024)
    gcol = ar.alloc(8, F32)
    bcol_o = ar.alloc(8)
    wtoks = []
    for nm, wbuf in (("wq", wq), ("wk", wk), ("wv", wv), ("wo", wo)):
        wtoks.append(em.dma("g", "dW", out=wbuf.rearrange("p (c n) -> p c n", c=8),
                            in_=dram["l1a_" + nm].rearrange("(c p) n -> p c n", p=128),
                            deps=bar))
    wtoks.append(em.dma("g", "dW", out=bcol_o, in_=dram["l1a_beta"], deps=bar))
    dmask = ar.alloc(16 * 128)
    lamq = ar.alloc(16 * 128)
    lamk = ar.alloc(16 * 128)
    selc = ar.alloc(8 * 128)
    hs_bf = ar.alloc(8 * 16)
    hs_f = ar.alloc(8 * 16, F32)
    ctoks2 = [
        em.dma("q", "dX", out=gcol, in_=dram["l1a_gamma"], deps=bar),
        em.dma("q", "dX", out=dmask, in_=dram["l1a_dmask"], deps=bar),
        em.dma("q", "dX", out=lamq, in_=dram["l1a_lamq"], deps=bar),
        em.dma("q", "dX", out=lamk, in_=dram["l1a_lamk"], deps=bar),
        em.dma("q", "dX", out=selc[0:16, :], in_=dram["l1a_selc"], deps=bar),
        em.dma("q", "dX", out=hs_bf, in_=dram["l1a_hs_bf"], deps=bar),
        em.dma("q", "dX", out=hs_f, in_=dram["l1a_hs_f"], deps=bar),
    ]
    ctok = [("dX", em.cnt["dX"])]
    wtok_all = [("dW", em.cnt["dW"])] + ctok
    lam_pows = ctx.lam_pows

    qb = ar.alloc(8 * BW)
    kb = ar.alloc(8 * BW)
    vb = ar.alloc(8 * BW)
    xnb = ar.alloc(8 * BW)
    ob = xnb
    cmap_t = Tb(ar.alloc(BW, F32))
    smap_t = Tb(ar.alloc(BW, F32))
    spA = Tb(ar.alloc(BW, F32))
    spB = Tb(ar.alloc(BW, F32))
    sc = spA
    state_f = ar.alloc(8 * 64, F32)
    state_bf = ar.alloc(8 * 64)
    ktl = Tb(ar.alloc(192))
    qtl = Tb(ar.alloc(128))
    vT = Tb(ar.alloc(64))
    aT = Tb(ar.alloc(128))
    oqb = Tb(ar.alloc(64))
    gnr = Tb(ar.alloc(2 * BW, F32))
    gnr_bf = Tb(ar.alloc(2 * BW))
    co = ar.alloc(8, F32)

    def xs2(ci, t):
        return x[:, xoff + ci * S + t * BW: xoff + ci * S + (t + 1) * BW]

    def xs512(ci, t):
        return x[:, xoff + ci * S + t * 512: xoff + ci * S + (t + 1) * 512]

    stats = ln_stats(ctx, xs512, dep_src=bar)

    t_co = None
    for m in range(8):
        for kc in range(NCH):
            dep = (wtok_all + ps.use(6)) if (m == 0 and kc == 0) else ()
            t_co = em.on("t", lambda eng, m=m, kc=kc: eng.matmul(
                ps.f32(6)[:, m:m + 1],
                wo[:, kc * 1024 + m * 128:kc * 1024 + (m + 1) * 128],
                bcol_o[:, kc:kc + 1], start=(kc == 0), stop=(kc == NCH - 1)),
                deps=dep, noinc=not (m == 7 and kc == NCH - 1))
    ps.set_wr(6, t_co)
    t_cov = em.on("v", lambda eng: eng.tensor_copy(out=co[:, :], in_=ps.f32(6)[:, 0:8]),
                  deps=[t_co])
    ps.read(6, t_cov)
    gw_toks = []
    for ci in range(NCH):
        tk = em.on("v", lambda eng, ci=ci: eng.tensor_scalar_mul(
            wo[:, ci * 1024:(ci + 1) * 1024], wo[:, ci * 1024:(ci + 1) * 1024],
            gcol[:, ci:ci + 1]), deps=wtok_all + [t_co, t_cov])
        gw_toks.append(tk)

    SCB, TRB, AVB, STB, OTB = 2, 3, 4, 5, 6
    state_cp = [None] * 16
    band_snap = []
    for t in range(NB):
        xn_toks = ln_normalize_tile(
            ctx, stats, xs2, dep_src=list(band_snap) + bar, t=t,
            xn_dst=lambda ci: xnb[:, ci * BW:(ci + 1) * BW], sc=sc, tw=BW)
        t_cm = em.dma("q", "dX", out=cmap_t.ap[:, :],
                      in_=dram["l1a_cmap"][:, t * BW:(t + 1) * BW],
                      deps=cmap_t.wdeps() + smap_t.wdeps() + band_snap + ctok)
        t_sm = em.dma("q", "dX", out=smap_t.ap[:, :],
                      in_=dram["l1a_smap"][:, t * BW:(t + 1) * BW], deps=())
        t_grp = ("dX", em.cnt["dX"])
        cmap_t.wrote(t_grp)
        smap_t.wrote(t_grp)

        def proj(wbuf, dst):
            outs = []

            def evac(m, bi, mmtok):
                d1 = em.on("s", lambda eng, m=m, bi=bi: eng.activation(
                    out=dst[:, m * BW:(m + 1) * BW], in_=ps.f32(bi, BW),
                    func=AF.Identity), deps=[mmtok])
                ps.read(bi, d1)
                outs.append(d1)
            linear_groups(ctx, lambda k, m, wbuf=wbuf: wbuf[:, k * 1024 + m * 128:k * 1024 + (m + 1) * 128],
                          lambda k: xnb[:, k * BW:(k + 1) * BW],
                          NCH, list(range(NCH)),
                          deps=xn_toks + wtok_all + list(band_snap), banks=[2, 4, 5],
                          evac=evac, n=BW)
            return outs

        qtoks = proj(wq, qb)
        ktoks = proj(wk, kb)
        vtoks = proj(wv, vb)
        pdeps = [qtoks[-1], ktoks[-1], vtoks[-1]]
        rope_toks = []
        for ci in range(NCH):
            for buf in (qb, kb):
                t_r = emit_rope(ctx, buf[:, ci * BW:(ci + 1) * BW], SCB,
                                cmap_t.ap, smap_t.ap, spA, spB,
                                deps=pdeps + rope_toks[-1:], map_deps=[t_grp])
                rope_toks.append(t_r)
        cmap_t.read(rope_toks[-1])
        smap_t.read(rope_toks[-1])

        ob_toks = []
        for c in range(BW // 128):
            cg = t * (BW // 128) + c
            last_ot = None
            for h in range(16):
                bh = (h % 2) * 64
                cb = h // 2
                q_ap = qb[bh:bh + 64, cb * BW + c * 128: cb * BW + (c + 1) * 128]
                k_ap = kb[bh:bh + 64, cb * BW + c * 128: cb * BW + (c + 1) * 128]
                v_ap = vb[bh:bh + 64, cb * BW + c * 128: cb * BW + (c + 1) * 128]
                scol = (h // 4) * 64 + ((h // 2) % 2) * 256
                st_f = state_f[bh:bh + 64, scol:scol + 64]
                st_bf = state_bf[bh:bh + 64, scol:scol + 64]

                t_s = em.on("t", lambda eng, k_ap=k_ap, q_ap=q_ap: eng.matmul(
                    ps.f32(SCB)[:, 0:128], k_ap, q_ap, start=True, stop=True),
                    deps=rope_toks + ps.use(SCB))
                ps.set_wr(SCB, t_s)
                t_mask = em.on("v", lambda eng, h=h: eng.tensor_mul(
                    aT.ap[:, 0:128], ps.f32(SCB)[:, 0:128],
                    dmask[:, h * 128:(h + 1) * 128]),
                    deps=[t_s] + aT.wdeps() + ctok)
                aT.wrote(t_mask)
                ps.read(SCB, t_mask)

                t_vt = em.on("t", lambda eng, v_ap=v_ap, bh=bh: eng.transpose(
                    ps.bf16(TRB)[:, 0:64], v_ap, b["ident"][bh:bh + 64, bh:bh + 64]),
                    deps=pdeps + ps.use(TRB))
                ps.set_wr(TRB, t_vt)
                t_vte = em.on("s", lambda eng: eng.copy(
                    out=vT.ap[:, 0:64], in_=ps.bf16(TRB)[:, 0:64]),
                    deps=[t_vt] + vT.wdeps())
                vT.wrote(t_vte)
                ps.read(TRB, t_vte)

                t_ktl = em.on("v", lambda eng, k_ap=k_ap, h=h, bh=bh: eng.tensor_mul(
                    ktl.ap[bh:bh + 64, 0:128], k_ap,
                    lamk[bh:bh + 64, h * 128:(h + 1) * 128]),
                    deps=rope_toks + ktl.wdeps() + ctok)
                ktl.wrote(t_ktl)
                t_kt = em.on("t", lambda eng, bh=bh: eng.transpose(
                    ps.bf16(TRB)[:, 64:128], ktl.ap[bh:bh + 64, 0:128],
                    b["ident"][bh:bh + 64, bh:bh + 64]),
                    deps=[t_ktl] + ps.use(TRB))
                ktl.read(t_kt)
                ps.set_wr(TRB, t_kt)
                t_kte = em.on("s", lambda eng: eng.copy(
                    out=ktl.ap[:, 128:192], in_=ps.bf16(TRB)[:, 64:128]),
                    deps=[t_kt] + ktl.wdeps())
                ktl.wrote(t_kte)
                ps.read(TRB, t_kte)

                t_qtl = em.on("v", lambda eng, q_ap=q_ap, h=h, bh=bh: eng.tensor_mul(
                    qtl.ap[bh:bh + 64, 0:128], q_ap,
                    lamq[bh:bh + 64, h * 128:(h + 1) * 128]),
                    deps=rope_toks + qtl.wdeps() + ctok)
                qtl.wrote(t_qtl)

                dep_o = [t_mask, t_vte] + ps.use(AVB)
                if cg > 0:
                    dep_o.append(state_cp[h])
                t_o1 = em.on("t", lambda eng, cg=cg: eng.matmul(
                    ps.f32(AVB)[:, 0:64], aT.ap[:, 0:128],
                    vT.ap[:, 0:64],
                    start=True, stop=(cg == 0)), deps=dep_o, noinc=(cg != 0))
                if cg > 0:
                    t_o1 = em.on("t", lambda eng, st_bf=st_bf, bh=bh: eng.matmul(
                        ps.f32(AVB)[:, 0:64],
                        qtl.ap[bh:bh + 64, 0:128], st_bf,
                        start=False, stop=True), deps=[t_qtl])
                aT.read(t_o1)
                vT.read(t_o1)
                qtl.read(t_o1)
                ps.set_wr(AVB, t_o1)

                t_su = em.on("t", lambda eng, bh=bh: eng.matmul(
                    ps.f32(STB)[bh:bh + 64, 0:64], ktl.ap[:, 128:192],
                    vT.ap[:, 0:64], start=True, stop=True),
                    deps=[t_kte, t_vte] + ps.use(STB))
                ktl.read(t_su)
                vT.read(t_su)
                ps.set_wr(STB, t_su)
                lam_c = lam_pows[h]
                if cg == 0:
                    t_acc = em.on("v", lambda eng, st_f=st_f, bh=bh: eng.tensor_copy(
                        out=st_f, in_=ps.f32(STB)[bh:bh + 64, 0:64]), deps=[t_su])
                else:
                    t_acc = em.on("v", lambda eng, st_f=st_f, lam_c=lam_c, bh=bh:
                                  eng.scalar_tensor_tensor(
                                      out=st_f, in0=st_f, scalar=lam_c,
                                      in1=ps.f32(STB)[bh:bh + 64, 0:64],
                                      op0=ALU.mult, op1=ALU.add),
                                  deps=[t_su, state_cp[h]])
                ps.read(STB, t_acc)
                t_cp = em.on("v", lambda eng, st_f=st_f, st_bf=st_bf: eng.tensor_copy(
                    out=st_bf, in_=st_f), deps=[t_acc])
                state_cp[h] = t_cp

                t_dv = em.on("v", lambda eng: eng.tensor_copy(
                    out=oqb.ap[:, 0:64], in_=ps.f32(AVB)[:, 0:64]),
                    deps=[t_o1] + oqb.wdeps())
                oqb.wrote(t_dv)
                ps.read(AVB, t_dv)
                t_ot = em.on("t", lambda eng, bh=bh, cb=cb: eng.transpose(
                    ps.bf16(OTB)[bh:bh + 64, cb * 128:(cb + 1) * 128],
                    oqb.ap[:, 0:64], b["ident"]),
                    deps=[t_dv] + (ps.use(OTB) if h == 0 else []))
                oqb.read(t_ot)
                ps.set_wr(OTB, t_ot)
                last_ot = t_ot
            t_oe = em.on("v", lambda eng, c=c: eng.tensor_copy(
                out=ob.rearrange("p (cc tt) -> p cc tt", cc=8)[:, :, c * 128:(c + 1) * 128],
                in_=ps.bf16(OTB).rearrange("p (cc tt) -> p cc tt", cc=8)),
                deps=[last_ot])
            ps.read(OTB, t_oe)
            ob_toks.append(t_oe)

        t_sum = None
        for ci in range(NCH):
            dep = (ob_toks + ps.use(7) + ctok) if ci == 0 else ()
            t_sum = em.on("t", lambda eng, ci=ci: eng.matmul(
                ps.f32(7)[0:16, 0:BW], hs_bf[:, ci * 16:(ci + 1) * 16],
                ob[:, ci * BW:(ci + 1) * BW],
                start=(ci == 0), stop=(ci == NCH - 1)),
                deps=dep, noinc=(ci != NCH - 1))
        ps.set_wr(7, t_sum)
        t_mean = em.on("v", lambda eng: eng.tensor_scalar_mul(
            gnr.ap[0:16, 0:BW], ps.f32(7)[0:16, 0:BW], 1.0 / 64), deps=[t_sum] + gnr.wdeps())
        ps.read(7, t_mean)
        t_sq = None
        sq_prev = None
        for ci in range(NCH):
            t_q = em.on("s", lambda eng, ci=ci: eng.activation(
                out=b["xsq"][:, 0:BW], in_=ob[:, ci * BW:(ci + 1) * BW],
                func=AF.Square), deps=ob_toks + ([sq_prev] if sq_prev else []))
            dep = [t_q] + (ps.use(7) if ci == 0 else [])
            t_sq = em.on("t", lambda eng, ci=ci: eng.matmul(
                ps.f32(7)[0:16, 0:BW], hs_f[:, ci * 16:(ci + 1) * 16],
                b["xsq"][:, 0:BW],
                start=(ci == 0), stop=(ci == NCH - 1)), deps=dep)
            sq_prev = t_sq
        ps.set_wr(7, t_sq)
        t_ex2 = em.on("v", lambda eng: eng.tensor_scalar_mul(
            gnr.ap[0:16, BW:2 * BW], ps.f32(7)[0:16, 0:BW], 1.0 / 64), deps=[t_sq])
        ps.read(7, t_ex2)
        t_msq = em.on("v", lambda eng: eng.tensor_mul(
            spB.ap[0:16, 0:BW], gnr.ap[0:16, 0:BW], gnr.ap[0:16, 0:BW]),
            deps=[t_mean, t_ex2] + spB.wdeps())
        spB.wrote(t_msq)
        t_var = em.on("v", lambda eng: eng.tensor_sub(
            gnr.ap[0:16, BW:2 * BW], gnr.ap[0:16, BW:2 * BW], spB.ap[0:16, 0:BW]),
            deps=[t_msq])
        spB.read(t_var)
        t_ln = em.on("s", lambda eng: eng.activation(
            out=spB.ap[0:16, 0:BW], in_=gnr.ap[0:16, BW:2 * BW], func=AF.Ln,
            bias=EPS), deps=[t_var] + spB.wdeps())
        spB.wrote(t_ln)
        t_r = em.on("s", lambda eng: eng.activation(
            out=gnr.ap[0:16, BW:2 * BW], in_=spB.ap[0:16, 0:BW], func=AF.Exp,
            scale=-0.5), deps=[t_ln])
        spB.read(t_r)
        gnr.wrote(t_r)
        t_bf = em.on("v", lambda eng: eng.tensor_copy(
            out=gnr_bf.ap[0:16, 0:2 * BW], in_=gnr.ap[0:16, 0:2 * BW]),
            deps=[t_r, t_mean] + gnr_bf.wdeps())
        gnr_bf.wrote(t_bf)
        gn_toks = []
        for ci in range(NCH):
            dep = ([t_bf] + ps.use(2) + ps.use(4) + ctok)
            t_mb = em.on("t", lambda eng, ci=ci: eng.matmul(
                ps.f32(2, BW), selc[0:16, ci * 128:(ci + 1) * 128],
                gnr_bf.ap[0:16, 0:BW], start=True, stop=True),
                deps=dep)
            ps.set_wr(2, t_mb)
            t_rb = em.on("t", lambda eng, ci=ci: eng.matmul(
                ps.f32(4, BW), selc[0:16, ci * 128:(ci + 1) * 128],
                gnr_bf.ap[0:16, BW:2 * BW], start=True, stop=True),
                deps=[t_bf] + ps.use(4))
            ps.set_wr(4, t_rb)
            t_s1 = em.on("v", lambda eng, ci=ci: eng.tensor_sub(
                spA.ap[:, 0:BW], ob[:, ci * BW:(ci + 1) * BW], ps.f32(2, BW)),
                deps=[t_mb] + spA.wdeps())
            spA.wrote(t_s1)
            ps.read(2, t_s1)
            t_s2 = em.on("v", lambda eng, ci=ci: eng.tensor_mul(
                ob[:, ci * BW:(ci + 1) * BW], spA.ap[:, 0:BW], ps.f32(4, BW)),
                deps=[t_s1, t_rb])
            spA.read(t_s2)
            ps.read(4, t_s2)
            gn_toks.append(t_s2)

        def evac_o(m, bi, mmtok, t=t):
            x_ap = xs2(m, t)
            tk = em.on("v", lambda eng, x_ap=x_ap, m=m, bi=bi: eng.scalar_tensor_tensor(
                out=x_ap, in0=ps.f32(bi, BW), scalar=co[:, m:m + 1],
                in1=x_ap, op0=ALU.add, op1=ALU.add),
                deps=[mmtok, t_cov] + xn_toks + stats["xdone"])
            ps.read(bi, tk)
        linear_groups(ctx, lambda k, m: wo[:, k * 1024 + m * 128:k * 1024 + (m + 1) * 128],
                      lambda k: ob[:, k * BW:(k + 1) * BW],
                      NCH, list(range(NCH)), deps=gn_toks + gw_toks, banks=[2, 4],
                      evac=evac_o, n=BW)
        band_snap = em.snap()
    ar.restore()


SUB_BUILDERS["l1a"] = sub_l1a

def sub_l1b(ctx, dram, xoff):
    em, ps, ar, b, S, NT = ctx.em, ctx.ps, ctx.ar, ctx.b, ctx.S, ctx.NT
    x = b["x_res"]
    ar.save()
    bar = em.snap()
    wemb = ar.alloc(8 * 2048)
    wanc = ar.alloc(64)
    wout = ar.alloc(8 * 1024)
    bocol = ar.alloc(8, F32)
    wtoks = [
        em.dma("g", "dW", out=wemb.rearrange("p (c n) -> p c n", c=8),
               in_=dram["l1b_wemb"].rearrange("(c p) n -> p c n", p=128), deps=bar),
        em.dma("g", "dW", out=wanc, in_=dram["l1b_wanc"], deps=bar),
        em.dma("g", "dW", out=wout.rearrange("p (c n) -> p c n", c=8),
               in_=dram["l1b_wout"].rearrange("(c p) n -> p c n", p=128), deps=bar),
    ]
    ctoks2 = [em.dma("q", "dX", out=bocol, in_=dram["l1b_bout"], deps=bar)]
    wtok_all = [("dW", em.cnt["dW"]), ("dX", em.cnt["dX"])]

    qkb = ar.alloc(8 * 512)
    vb = ar.alloc(8 * 512)
    xnb = ar.alloc(8 * 512)
    o3b = ar.alloc(8 * 512)
    vT = Tb(ar.alloc(66))
    attn_band = ar.alloc(64 * 64)
    ab_tb = Tb(attn_band)
    ab_toks = []
    atT = Tb(ar.alloc(128))
    out1 = ar.alloc(8 * 66, F32)
    out2 = ar.alloc(8 * 64)
    spA = Tb(ar.alloc(512, F32))
    spB = Tb(ar.alloc(512, F32))
    nmx = Tb(ar.alloc(16, F32))
    sc = spA

    t_ones = em.on("v", lambda eng: eng.memset(vT.ap[:, 64:65], 1.0), deps=bar)
    t_z1 = em.on("v", lambda eng: eng.memset(out1, 0.0), deps=bar)

    def xs(ci, t):
        return x[:, xoff + ci * S + t * 512: xoff + ci * S + (t + 1) * 512]

    stats = ln_stats(ctx, xs, dep_src=bar)

    SCB, TRB, O1B = 2, 3, 6
    out1_acc = [None] * 16
    band_snap = []
    for t in range(NT):
        xn_toks = ln_normalize_tile(
            ctx, stats, xs, dep_src=list(band_snap) + bar, t=t,
            xn_dst=lambda ci: xnb[:, ci * 512:(ci + 1) * 512], sc=sc)

        def proj(dst, off):
            outs = []

            def evac(m, bi, mmtok):
                d1 = em.on("s", lambda eng, m=m, bi=bi: eng.activation(
                    out=dst[:, m * 512:(m + 1) * 512], in_=ps.f32(bi),
                    func=AF.Identity), deps=[mmtok])
                ps.read(bi, d1)
                outs.append(d1)
            linear_groups(ctx,
                          lambda k, m, off=off: wemb[:, k * 2048 + off + m * 128:
                                                     k * 2048 + off + (m + 1) * 128],
                          lambda k: xnb[:, k * 512:(k + 1) * 512],
                          NCH, list(range(NCH)),
                          deps=xn_toks + wtok_all + list(band_snap),
                          banks=[2, 4, 5], evac=evac)
            return outs

        qtoks = proj(qkb, 0)
        vtoks = proj(vb, 1024)
        pdeps = [qtoks[-1], vtoks[-1]]

        for h in range(16):
            bh = (h % 2) * 64
            cb = h // 2
            o1b = 6 + (h % 2)
            for tt in range(4):
                qk_ap = qkb[bh:bh + 64, cb * 512 + tt * 128: cb * 512 + (tt + 1) * 128]
                v_ap = vb[bh:bh + 64, cb * 512 + tt * 128: cb * 512 + (tt + 1) * 128]
                gt = t * 4 + tt
                t_s = em.on("t", lambda eng, qk_ap=qk_ap, bh=bh: eng.matmul(
                    ps.f32(SCB)[:, 0:64], qk_ap, wanc[bh:bh + 64, 0:64],
                    start=True, stop=True), deps=pdeps + wtok_all + ps.use(SCB))
                ps.set_wr(SCB, t_s)
                t_mx = em.on("v", lambda eng: eng.reduce_max(
                    nmx.ap[:, 0:1], ps.f32(SCB)[:, 0:64], axis=AX.X, negate=True),
                    deps=[t_s] + nmx.wdeps())
                t_m8 = em.on("v", lambda eng: eng.tensor_scalar_mul(
                    nmx.ap[:, 1:2], nmx.ap[:, 0:1], 0.125), deps=[t_mx])
                nmx.wrote(t_m8)
                t_ex = em.on("s", lambda eng: eng.activation(
                    out=spB.ap.bitcast(BF16)[:, 0:64], in_=ps.f32(SCB)[:, 0:64],
                    func=AF.Exp, scale=0.125, bias=nmx.ap[:, 1:2],
                    accum_out=nmx.ap[:, 2:3]),
                    deps=[t_m8] + spB.wdeps())
                spB.wrote(t_ex)
                nmx.wrote(t_ex)
                ps.read(SCB, t_ex)
                t_rl = em.on("v", lambda eng: eng.reciprocal(
                    nmx.ap[:, 3:4], nmx.ap[:, 2:3]), deps=[t_ex])
                nmx.wrote(t_rl)
                a_dst = attn_band[:, (h * 4 + tt) * 64:(h * 4 + tt + 1) * 64]
                t_at = em.on("v", lambda eng, a_dst=a_dst: eng.tensor_scalar(
                    out=a_dst, in0=spB.ap.bitcast(BF16)[:, 0:64],
                    scalar1=nmx.ap[:, 3:4], scalar2=None, op0=ALU.mult),
                    deps=[t_rl] + ab_tb.wdeps())
                spB.read(t_at)
                nmx.read(t_at)
                ab_toks.append(t_at)
                t_vt = em.on("t", lambda eng, v_ap=v_ap, bh=bh: eng.transpose(
                    ps.bf16(TRB)[:, 0:64], v_ap, b["ident"][bh:bh + 64, bh:bh + 64]),
                    deps=pdeps + ps.use(TRB))
                ps.set_wr(TRB, t_vt)
                t_ve = em.on("s", lambda eng: eng.copy(
                    out=vT.ap[:, 0:64], in_=ps.bf16(TRB)[:, 0:64]),
                    deps=[t_vt, t_ones] + vT.wdeps())
                vT.wrote(t_ve)
                ps.read(TRB, t_ve)
                o1 = ps.f32(o1b)[bh:bh + 64, 0:65]
                dep_o = [t_at, t_ve] + (ps.use(o1b) if tt == 0 else [])
                t_o1 = em.on("t", lambda eng, o1=o1, a_dst=a_dst, tt=tt: eng.matmul(
                    o1, a_dst, vT.ap[:, 0:65],
                    start=(tt == 0), stop=(tt == 3)), deps=dep_o,
                    noinc=(tt != 3))
                vT.read(t_o1)
            ps.set_wr(o1b, t_o1)
            ab_tb.read(t_o1)
            st_f = out1[bh:bh + 64, (h // 2) * 66:(h // 2) * 66 + 65]
            if t == 0:
                t_ac = em.on("v", lambda eng, st_f=st_f, o1=o1: eng.tensor_copy(
                    out=st_f, in_=o1), deps=[t_o1, t_z1])
            else:
                t_ac = em.on("v", lambda eng, st_f=st_f, o1=o1: eng.tensor_add(
                    st_f, st_f, o1), deps=[t_o1, out1_acc[h]])
            ps.read(o1b, t_ac)
            out1_acc[h] = t_ac
        t_spb = em.dma("q", "dS", out=dram["scr_out"][:, t * 4096:(t + 1) * 4096],
                       in_=attn_band[:, :], deps=ab_toks)
        ab_tb.wrote(t_spb)
        ab_toks = []
        band_snap = em.snap()


    out2_toks = []
    for h in range(16):
        bh = (h % 2) * 64
        st_f = out1[bh:bh + 64, (h // 2) * 66:(h // 2) * 66 + 65]
        o2 = out2[bh:bh + 64, (h // 2) * 64:(h // 2) * 64 + 64]
        t_1 = em.on("v", lambda eng, st_f=st_f, bh=bh: eng.tensor_scalar_max(
            spA.ap[bh:bh + 64, 0:1],
            st_f[:, 64:65], 1e-38), deps=[out1_acc[h]] + spA.wdeps())
        spA.wrote(t_1)
        t_2 = em.on("v", lambda eng, bh=bh: eng.reciprocal(
            spA.ap[bh:bh + 64, 1:2], spA.ap[bh:bh + 64, 0:1]), deps=[t_1])
        spA.wrote(t_2)
        t_3 = em.on("v", lambda eng, st_f=st_f, bh=bh: eng.tensor_single_scalar(
            out=spA.ap[bh:bh + 64, 2:3], in_=st_f[:, 64:65], scalar=0.0,
            op=ALU.is_gt), deps=[out1_acc[h], t_2])
        spA.wrote(t_3)
        t_4 = em.on("v", lambda eng, bh=bh: eng.tensor_mul(
            spA.ap[bh:bh + 64, 3:4], spA.ap[bh:bh + 64, 1:2],
            spA.ap[bh:bh + 64, 2:3]), deps=[t_3])
        spA.wrote(t_4)
        t_5 = em.on("v", lambda eng, st_f=st_f, o2=o2, bh=bh: eng.tensor_scalar(
            out=o2, in0=st_f[:, 0:64], scalar1=spA.ap[bh:bh + 64, 3:4],
            scalar2=None, op0=ALU.mult), deps=[t_4])
        spA.read(t_5)
        out2_toks.append(t_5)

    for t in range(NT):
        t_ld = em.dma("q", "dX", out=qkb[:, :],
                      in_=dram["scr_out"][:, t * 4096:(t + 1) * 4096],
                      deps=em.snap("t") + [("dS", em.cnt["dS"])])
        o3_toks = []
        for tt in range(4):
            for h in range(16):
                bh = (h % 2) * 64
                cb = h // 2
                attn_ap = qkb[:, (h * 4 + tt) * 64:(h * 4 + tt + 1) * 64]
                t_tr = em.on("t", lambda eng, attn_ap=attn_ap, bh=bh: eng.transpose(
                    ps.bf16(TRB)[bh:bh + 64, 0:128], attn_ap, b["ident"]),
                    deps=[t_ld] + out2_toks + ps.use(TRB))
                ps.set_wr(TRB, t_tr)
                t_te = em.on("s", lambda eng, bh=bh: eng.copy(
                    out=atT.ap[bh:bh + 64, 0:128],
                    in_=ps.bf16(TRB)[bh:bh + 64, 0:128]),
                    deps=[t_tr] + atT.wdeps())
                atT.wrote(t_te)
                ps.read(TRB, t_te)
                o2 = out2[bh:bh + 64, (h // 2) * 64:(h // 2) * 64 + 64]
                dst = ps.tens[1][bh:bh + 64, cb * 128:(cb + 1) * 128]
                t_o3 = em.on("t", lambda eng, o2=o2, dst=dst, bh=bh: eng.matmul(
                    dst, o2, atT.ap[bh:bh + 64, 0:128],
                    start=True, stop=True),
                    deps=[t_te] + out2_toks + (ps.use(4) + ps.use(5) if h == 0 else []))
                atT.read(t_o3)
                ps.set_wr(4, t_o3)
                ps.set_wr(5, t_o3)
            t_oe = em.on("v", lambda eng, tt=tt: eng.tensor_copy(
                out=o3b.rearrange("p (cc k) -> p cc k", cc=8)[:, :, tt * 128:(tt + 1) * 128],
                in_=ps.tens[1].bitcast(F32)[:, 0:1024] if False else ps.tens[1][:, 0:1024]),
                deps=[t_o3])
            ps.read(4, t_oe)
            ps.read(5, t_oe)
            o3_toks.append(t_oe)

        def evac_o(m, bi, mmtok, t=t):
            x_ap = xs(m, t)
            tk = em.on("v", lambda eng, x_ap=x_ap, m=m, bi=bi: eng.scalar_tensor_tensor(
                out=x_ap, in0=ps.f32(bi), scalar=bocol[:, m:m + 1],
                in1=x_ap, op0=ALU.add, op1=ALU.add),
                deps=[mmtok] + stats["xdone"] + ctoks2)
            ps.read(bi, tk)
        linear_groups(ctx, lambda k, m: wout[:, k * 1024 + m * 128:k * 1024 + (m + 1) * 128],
                      lambda k: o3b[:, k * 512:(k + 1) * 512],
                      NCH, list(range(NCH)), deps=o3_toks, banks=[2, 6], evac=evac_o)
    ar.restore()


SUB_BUILDERS["l1b"] = sub_l1b

def emit_wquant_l1f(ctx, dram):
    em, ps, ar, b = ctx.em, ctx.ps, ctx.ar, ctx.b
    ar.save()
    bar = em.snap()
    wf = Tb(ar.alloc(2816, F32))
    wq = Tb(ar.alloc(2816))
    wt = Tb(ar.alloc(2816))
    qs = {"gm": Tb(ar.alloc(3 * 96, F32)), "t": Tb(ar.alloc(1408, F32)),
          "hi": Tb(ar.alloc(1408, F32)), "lo": Tb(ar.alloc(1408, F32)),
          "mk": Tb(ar.alloc(1408, F32))}
    TRB = 6
    for wi, wname in enumerate(("l1f_wgate", "l1f_wup")):
        for nc_ in range(22):
            t_ld = em.dma("q", "dX", out=wf.ap[:, 0:1024],
                          in_=dram[wname][nc_ * 128:(nc_ + 1) * 128, :],
                          deps=wf.wdeps() + bar)
            wf.wrote(t_ld)
            tq1 = emit_quant(ctx, wf.ap[:, 0:1024], wq.ap[:, 0:1024], qs, [t_ld] + wq.wdeps())
            wf.read(tq1)
            wq.wrote(tq1)
            ev = None
            for kc in range(8):
                t_tr = em.on("t", lambda eng, kc=kc: eng.transpose(
                    ps.bf16(TRB)[:, kc * 128:(kc + 1) * 128],
                    wq.ap[:, kc * 128:(kc + 1) * 128], b["ident"]),
                    deps=[tq1] + (ps.use(TRB) if kc == 0 else []))
                wq.read(t_tr)
                ps.set_wr(TRB, t_tr)
            ev = em.on("v", lambda eng: eng.tensor_copy(
                out=wt.ap[:, 0:1024], in_=ps.bf16(TRB)[:, 0:1024]),
                deps=[t_tr] + wt.wdeps())
            ps.read(TRB, ev)
            wt.wrote(ev)
            t_st = em.dma("q", "dS",
                          out=dram["scr_wq"].rearrange("p (z q) -> p z q", z=3)
                          [:, wi, :].rearrange("p (kc n) -> p kc n", kc=8)
                          [:, :, nc_ * 128:(nc_ + 1) * 128],
                          in_=wt.ap[:, 0:1024].rearrange("p (kc n) -> p kc n", kc=8),
                          deps=[ev])
            wt.read(t_st)
    for mc8 in range(8):
        t_ld = em.dma("q", "dX", out=wf.ap[:, 0:2816],
                      in_=dram["l1f_wdown"][mc8 * 128:(mc8 + 1) * 128, :],
                      deps=wf.wdeps() + bar)
        wf.wrote(t_ld)
        tq1 = emit_quant(ctx, wf.ap[:, 0:1408], wq.ap[:, 0:1408], qs, [t_ld] + wq.wdeps())
        tq2 = emit_quant(ctx, wf.ap[:, 1408:2816], wq.ap[:, 1408:2816], qs, [tq1])
        wf.read(tq2)
        wq.wrote(tq2)
        for half in range(2):
            ev = None
            for kk in range(11):
                kcn = half * 11 + kk
                t_tr = em.on("t", lambda eng, kcn=kcn, kk=kk: eng.transpose(
                    ps.bf16(TRB)[:, (kk % 8) * 128:(kk % 8) * 128 + 128],
                    wq.ap[:, kcn * 128:(kcn + 1) * 128], b["ident"]),
                    deps=[tq2] + (ps.use(TRB) if kk % 8 == 0 else []))
                wq.read(t_tr)
                ps.set_wr(TRB, t_tr)
                if kk % 8 == 7 or kk == 10:
                    lo8 = (kk // 8) * 8
                    ev = em.on("v", lambda eng, lo8=lo8, kk=kk, half=half: eng.tensor_copy(
                        out=wt.ap[:, (half * 11 + lo8) * 128:(half * 11 + kk + 1) * 128],
                        in_=ps.bf16(TRB)[:, 0:(kk - lo8 + 1) * 128]),
                        deps=[t_tr] + (wt.wdeps() if (half == 0 and lo8 == 0) else []))
                    ps.read(TRB, ev)
            wt.wrote(ev)
        t_st = em.dma("q", "dS",
                      out=dram["scr_wq"].rearrange("p (z q) -> p z q", z=3)
                      [:, 2, :].rearrange("p (kcn m) -> p kcn m", kcn=22)
                      [:, :, mc8 * 128:(mc8 + 1) * 128],
                      in_=wt.ap[:, 0:22 * 128].rearrange("p (kcn m) -> p kcn m", kcn=22),
                      deps=[ev])
        wt.read(t_st)
    ar.restore()
    return [("dS", em.cnt["dS"])]


def sub_l1f(ctx, dram, xoff):
    em, ps, ar, b, S, NT = ctx.em, ctx.ps, ctx.ar, ctx.b, ctx.S, ctx.NT
    x = b["x_res"]
    ar.save()
    bar = em.snap()
    TW = 256
    NTT = S // TW
    h_sb = [Tb(ar.alloc(TW, F32)), Tb(ar.alloc(TW, F32))]
    sg = Tb(ar.alloc(TW, F32))
    h_tm = Tb(ar.alloc(2 * 1408, F32))
    hq_tm = Tb(ar.alloc(2 * 1408))
    hq = Tb(ar.alloc(11 * TW))
    qs = {"gm": Tb(ar.alloc(3 * 64, F32)), "t": Tb(ar.alloc(704, F32)),
          "hi": Tb(ar.alloc(704, F32)), "lo": Tb(ar.alloc(704, F32)),
          "mk": Tb(ar.alloc(704, F32))}
    sums = Tb(ar.alloc(8, F32))
    ar.save()
    x_tm = Tb(ar.alloc(1024, F32))
    xq_tm = Tb(ar.alloc(1024))
    xqs = [Tb(ar.alloc(1024)), Tb(ar.alloc(1024))]

    TRB = 6

    def xs(ci, t):
        return x[:, xoff + ci * S + t * 512: xoff + ci * S + (t + 1) * 512]

    xq_sp = []
    for tc in range(S // 128):
        ev_toks = []
        for ci in range(NCH):
            x_ap = x[:, xoff + ci * S + tc * 128: xoff + ci * S + (tc + 1) * 128]
            t_tr = em.on("t", lambda eng, x_ap=x_ap, ci=ci: eng.transpose(
                ps.f32(4 + (ci // 4))[:, (ci % 4) * 128:(ci % 4) * 128 + 128],
                x_ap, b["ident_f"]),
                deps=bar + xq_sp[-1:] + ps.use(4 + ci // 4))
            ps.set_wr(4 + ci // 4, t_tr)
            if ci % 4 == 3:
                t_ev = em.on("v", lambda eng, ci=ci: eng.tensor_copy(
                    out=x_tm.ap[:, (ci - 3) * 128:(ci + 1) * 128],
                    in_=ps.f32(4 + ci // 4)[:, 0:512]),
                    deps=[t_tr] + (x_tm.wdeps() if ci == 3 else []))
                ps.read(4 + ci // 4, t_ev)
                ev_toks.append(t_ev)
        x_tm.wrote(ev_toks[-1])
        t_sum = em.on("v", lambda eng: eng.reduce_sum(
            sums.ap[:, 0:1], x_tm.ap[:, 0:1024], axis=AX.X), deps=ev_toks + sums.wdeps())
        t_sq = em.on("s", lambda eng: eng.activation(
            out=h_tm.ap[:, 0:1024], in_=x_tm.ap[:, 0:1024], func=AF.Square,
            accum_out=sums.ap[:, 1:2]), deps=ev_toks + h_tm.wdeps() + [t_sum])
        h_tm.wrote(t_sq)
        sums.wrote(t_sq)
        t_nm = em.on("v", lambda eng: eng.tensor_scalar_mul(
            sums.ap[:, 2:3], sums.ap[:, 0:1], -1.0 / D), deps=[t_sq])
        t_e2 = em.on("v", lambda eng: eng.tensor_scalar_mul(
            sums.ap[:, 3:4], sums.ap[:, 1:2], 1.0 / D), deps=[t_nm])
        t_mq = em.on("v", lambda eng: eng.tensor_mul(
            sums.ap[:, 4:5], sums.ap[:, 2:3], sums.ap[:, 2:3]), deps=[t_e2])
        t_vr = em.on("v", lambda eng: eng.tensor_sub(
            sums.ap[:, 5:6], sums.ap[:, 3:4], sums.ap[:, 4:5]), deps=[t_mq])
        t_ln = em.on("s", lambda eng: eng.activation(
            out=sums.ap[:, 6:7], in_=sums.ap[:, 5:6], func=AF.Ln, bias=EPS),
            deps=[t_vr])
        t_r = em.on("s", lambda eng: eng.activation(
            out=sums.ap[:, 7:8], in_=sums.ap[:, 6:7], func=AF.Exp, scale=-0.5),
            deps=[t_ln])
        sums.wrote(t_r)
        t_xn = em.on("v", lambda eng: eng.tensor_scalar(
            out=x_tm.ap[:, 0:1024], in0=x_tm.ap[:, 0:1024],
            scalar1=sums.ap[:, 2:3], scalar2=sums.ap[:, 7:8],
            op0=ALU.add, op1=ALU.mult), deps=[t_r] + ev_toks)
        x_tm.wrote(t_xn)
        sums.read(t_xn)
        t_qa = emit_quant(ctx, x_tm.ap[:, 0:512], xq_tm.ap[:, 0:512], qs,
                          [t_xn] + xq_tm.wdeps())
        t_q = emit_quant(ctx, x_tm.ap[:, 512:1024], xq_tm.ap[:, 512:1024], qs,
                         [t_xn, t_qa])
        x_tm.read(t_q)
        xq_tm.wrote(t_q)
        xpar = tc % 2
        ev2 = []
        for ci in range(NCH):
            t_tr = em.on("t", lambda eng, ci=ci: eng.transpose(
                ps.bf16(TRB)[:, (ci % 4) * 128:(ci % 4) * 128 + 128],
                xq_tm.ap[:, ci * 128:(ci + 1) * 128], b["ident"]),
                deps=[t_q] + ps.use(TRB))
            xq_tm.read(t_tr)
            ps.set_wr(TRB, t_tr)
            if ci % 4 == 3:
                t_ev = em.on("v", lambda eng, ci=ci, xpar=xpar: eng.tensor_copy(
                    out=xqs[xpar].ap[:, (ci - 3) * 128:(ci + 1) * 128],
                    in_=ps.bf16(TRB)[:, 0:512]),
                    deps=[t_tr] + (xqs[xpar].wdeps() if ci == 3 else []))
                ps.read(TRB, t_ev)
                ev2.append(t_ev)
        xqs[xpar].wrote(ev2[-1])
        t_sp = em.dma("q", "dS",
                      out=dram["scr_xn"].rearrange("p (c tt) -> p c tt", c=8)
                      [:, :, tc * 128:(tc + 1) * 128],
                      in_=xqs[xpar].ap[:, 0:1024].rearrange("p (c tt) -> p c tt", c=8),
                      deps=ev2)
        xqs[xpar].read(t_sp)
        xq_sp.append(t_sp)
    xq_done = [("dS", em.cnt["dS"])]
    ctx.mark("l1f.p1")

    ar.restore()
    p2snap = em.snap()
    wg_all = ar.alloc(8 * 1408)
    wu_all = ar.alloc(8 * 1408)
    wd_all = ar.alloc(11 * 1024)
    xq_t = Tb(ar.alloc(8 * TW))
    wgu_rd = []
    for half in range(2):
        hw = half * 11
        t_wg = em.dma("q", "dQ", out=wg_all.rearrange("p (kc n) -> p kc n", kc=8),
                      in_=dram["scr_wq"].rearrange("p (z q) -> p z q", z=3)[:, 0, :]
                      .rearrange("p (kc n) -> p kc n", kc=8)
                      [:, :, hw * 128:(hw + 11) * 128],
                      deps=p2snap + xq_done + wgu_rd)
        t_wd = em.dma("q", "dQ", out=wd_all,
                      in_=dram["scr_wq"].rearrange("p (z q) -> p z q", z=3)[:, 2, :]
                      [:, hw * 1024:(hw + 11) * 1024],
                      deps=())
        t_wu = em.dma("q", "dQb", out=wu_all.rearrange("p (kc n) -> p kc n", kc=8),
                      in_=dram["scr_wq"].rearrange("p (z q) -> p z q", z=3)[:, 1, :]
                      .rearrange("p (kc n) -> p kc n", kc=8)
                      [:, :, hw * 128:(hw + 11) * 128],
                      deps=p2snap + xq_done + wgu_rd)
        wtok = [("dQ", em.cnt["dQ"]), ("dQb", em.cnt["dQb"])]
        for tt in range(NTT):
            t_xq = em.dma("q", "dX", out=xq_t.ap[:, 0:8 * TW],
                          in_=dram["scr_xn"].rearrange("p (c z) -> p c z", c=8)
                          [:, :, tt * TW:(tt + 1) * TW],
                          deps=xq_t.wdeps() + xq_done)
            xq_t.wrote(t_xq)
            xqd = [("dX", em.cnt["dX"])]
            hq_parts = []
            for m in range(11):
                tokg = None
                for k in range(NCH):
                    dep = (wtok + xqd + ps.use(4)) if k == 0 else ()
                    tokg = em.on("t", lambda eng, k=k, m=m: eng.matmul(
                        ps.f32(4)[:, 0:TW], wg_all[:, k * 1408 + m * 128:k * 1408 + (m + 1) * 128],
                        xq_t.ap[:, k * TW:(k + 1) * TW],
                        start=(k == 0), stop=(k == NCH - 1)),
                        deps=dep, noinc=(k != NCH - 1))
                ps.set_wr(4, tokg)
                toku = None
                for k in range(NCH):
                    dep = (wtok + xqd + ps.use(5)) if k == 0 else ()
                    toku = em.on("t", lambda eng, k=k, m=m: eng.matmul(
                        ps.f32(5)[:, 0:TW], wu_all[:, k * 1408 + m * 128:k * 1408 + (m + 1) * 128],
                        xq_t.ap[:, k * TW:(k + 1) * TW],
                        start=(k == 0), stop=(k == NCH - 1)),
                        deps=dep, noinc=(k != NCH - 1))
                ps.set_wr(5, toku)
                if m == 10:
                    xq_t.read(toku)
                spar = m % 2
                t_sg = em.on("s", lambda eng: eng.activation(
                    out=sg.ap[:, 0:TW], in_=ps.f32(4)[:, 0:TW], func=AF.Silu),
                    deps=[tokg] + sg.wdeps())
                sg.wrote(t_sg)
                ps.read(4, t_sg)
                t_h = em.on("v", lambda eng, spar=spar: eng.tensor_mul(
                    h_sb[spar].ap[:, 0:TW], sg.ap[:, 0:TW], ps.f32(5)[:, 0:TW]),
                    deps=[t_sg, toku] + h_sb[spar].wdeps())
                sg.read(t_h)
                h_sb[spar].wrote(t_h)
                ps.read(5, t_h)
                for z in range(TW // 128):
                    trb = 6 + (m % 2)
                    t_tr = em.on("t", lambda eng, z=z, m=m, trb=trb, spar=spar: eng.transpose(
                        ps.f32(trb)[:, 0:128],
                        h_sb[spar].ap[:, z * 128:(z + 1) * 128], b["ident_f"]),
                        deps=[t_h] + ps.use(trb))
                    h_sb[spar].read(t_tr)
                    ps.set_wr(trb, t_tr)
                    t_ev = em.on("v", lambda eng, z=z, m=m, trb=trb: eng.tensor_copy(
                        out=h_tm.ap[:, z * 1408 + m * 128:z * 1408 + (m + 1) * 128],
                        in_=ps.f32(trb)[:, 0:128]),
                        deps=[t_tr] + (h_tm.wdeps() if m == 0 and z == 0 else []))
                    ps.read(trb, t_ev)
                    hq_parts.append(t_ev)
            h_tm.wrote(hq_parts[-1])
            tq_toks = []
            for z in range(TW // 128):
                for c2 in (0, 704):
                    t_q = emit_quant(ctx, h_tm.ap[:, z * 1408 + c2:z * 1408 + c2 + 704],
                                     hq_tm.ap[:, z * 1408 + c2:z * 1408 + c2 + 704], qs,
                                     hq_parts + hq_tm.wdeps() + tq_toks[-1:])
                    tq_toks.append(t_q)
            hq_tm.wrote(tq_toks[-1])
            h_tm.read(tq_toks[-1])
            hv_toks = []
            for m in range(11):
                for z in range(TW // 128):
                    trb = 6 + (m % 2)
                    t_tr = em.on("t", lambda eng, m=m, z=z, trb=trb: eng.transpose(
                        ps.bf16(trb)[:, 0:128],
                        hq_tm.ap[:, z * 1408 + m * 128:z * 1408 + (m + 1) * 128],
                        b["ident"]),
                        deps=tq_toks + ps.use(trb))
                    hq_tm.read(t_tr)
                    ps.set_wr(trb, t_tr)
                    t_ev = em.on("v", lambda eng, m=m, z=z, trb=trb: eng.tensor_copy(
                        out=hq.ap[:, m * TW + z * 128:m * TW + (z + 1) * 128],
                        in_=ps.bf16(trb)[:, 0:128]),
                        deps=[t_tr] + (hq.wdeps() if (m == 0 and z == 0) else []))
                    ps.read(trb, t_ev)
                    hv_toks.append(t_ev)
            hq.wrote(hv_toks[-1])
            dn_done = []
            for k in range(11):
                for m in range(NCH):
                    bank = m // 2
                    cof = (m % 2) * TW
                    dep = wtok + hv_toks + (ps.use(bank) if k == 0 else [])
                    tok = em.on("t", lambda eng, k=k, m=m, bank=bank, cof=cof:
                                eng.matmul(
                        ps.tens[0][:, bank * 512 + cof:bank * 512 + cof + TW],
                        wd_all[:, k * 1024 + m * 128:k * 1024 + (m + 1) * 128],
                        hq.ap[:, k * TW:(k + 1) * TW],
                        start=(k == 0 and m % 2 == 0), stop=(k == 10 and m % 2 == 1)),
                        deps=dep if (k == 0 or m == 0) else (),
                        noinc=not (m == NCH - 1))
                    if m == NCH - 1:
                        hq.read(tok)
                    if k == 10:
                        dn_done.append((m, tok))
                        if m == NCH - 1 and tt == NTT - 1:
                            wgu_rd = [tok]
            last = dn_done[-1][1]
            for m, tok in dn_done:
                bank = m // 2
                cof = (m % 2) * TW
                ps.set_wr(bank, last)
                x_ap = x[:, xoff + m * S + (tt * TW):xoff + m * S + (tt + 1) * TW]
                t_add = em.on("v", lambda eng, x_ap=x_ap, bank=bank, cof=cof: eng.tensor_add(
                    x_ap, x_ap, ps.tens[0][:, bank * 512 + cof:bank * 512 + cof + TW]),
                    deps=[last] + xq_done)
                ps.read(bank, t_add)
        ctx.mark(f"l1f.h{half}")
    ar.restore()


SUB_BUILDERS["l1f"] = sub_l1f

W0F_OFF = {"w1": 0, "w2": 45056, "wqkv": 67584, "wo": 79872}


def emit_w0f_cast(ctx, dram):
    em, b = ctx.em, ctx.b
    xbf = b["x_res"].bitcast(BF16)
    scr = dram["scr_w0f"]
    pieces = []
    for layer in range(2):
        L = layer * 83968
        pieces.append((L + W0F_OFF["w1"], 8,
                       dram["l0f_w1"][layer].rearrange("(c p) n -> p c n", p=128)))
        pieces.append((L + W0F_OFF["wqkv"], 8,
                       dram["l0f_wqkv"][layer].rearrange("(c p) n -> p c n", p=128)))
        pieces.append((L + W0F_OFF["w2"], 22,
                       dram["l0f_w2"][layer].rearrange("(c p) n -> p c n", p=128)))
        pieces.append((L + W0F_OFF["wo"], 4,
                       dram["l0f_wo"][layer].rearrange("(c p) n -> p c n", p=128)))
    R = xbf.shape[1] // 2
    rbase = [0, R]
    war = [[], []]
    segs = []
    for off, c, src in pieces:
        n = src.shape[2]
        step = max(1, R // c)
        for n0 in range(0, n, step):
            w = min(step, n - n0)
            segs.append((off, c, n0, w, src))
    for i, (off, c, n0, w, src) in enumerate(segs):
        p = i % 2
        ncols = c * w
        t_ld = em.dma("g", "dC" if p == 0 else "dCb",
                      out=xbf[:, rbase[p]:rbase[p] + ncols]
                      .rearrange("p (c n) -> p c n", c=c),
                      in_=src[:, :, n0:n0 + w], deps=war[p])
        t_st = em.dma("q", "dQ" if p == 0 else "dQb",
                      out=scr[:, off:off + c * src.shape[2]]
                      .rearrange("p (c n) -> p c n", c=c)[:, :, n0:n0 + w],
                      in_=xbf[:, rbase[p]:rbase[p] + ncols], deps=[t_ld])
        war[p] = [t_st]
    ctx.w0f_cast = [("dQ", em.cnt["dQ"]), ("dQb", em.cnt["dQb"])]


def sub_l0f(ctx, dram, xoff):
    em, ps, ar, b, S, NT = ctx.em, ctx.ps, ctx.ar, ctx.b, ctx.S, ctx.NT
    x = b["x_res"]
    NB = NT
    ar.save()
    bar = em.snap()

    t_spx = em.dma("q", "dS", out=dram["scr_x"], in_=x[:, 0:NCH * S], deps=bar)
    spx = [("dS", em.cnt["dS"])]

    def xs(ci, t):
        return x[:, xoff + ci * S + t * 512: xoff + ci * S + (t + 1) * 512]

    spA = Tb(ar.alloc(512, F32))
    spB = Tb(ar.alloc(512, F32))
    spC = Tb(ar.alloc(512, F32))
    spD = Tb(ar.alloc(512, F32))
    stats = ln_stats(ctx, xs, dep_src=bar)
    xx_toks = []
    for t in range(NT):
        xx_toks += ln_normalize_tile(
            ctx, stats, xs, dep_src=bar + spx, t=t,
            xn_dst=lambda ci, t=t: xs(ci, t), sc=spA)

    ctx.mark("l0f.ln")
    qscol = ar.alloc(2, F32)
    kscol = ar.alloc(2, F32)
    lngc = ar.alloc(16, F32)
    lnbc = ar.alloc(16)
    b1c = ar.alloc(88, F32)
    b2c = ar.alloc(16, F32)
    hs_f = ar.alloc(4 * 16, F32)
    hs_bf = ar.alloc(4 * 16)
    selc = ar.alloc(4 * 128)
    ctks = [
        em.dma("q", "dX", out=qscol, in_=dram["l0f_qs"], deps=bar),
        em.dma("q", "dX", out=kscol, in_=dram["l0f_ks"], deps=bar),
        em.dma("q", "dX", out=lngc, in_=dram["l0f_lng"], deps=bar),
        em.dma("g", "dW", out=lnbc, in_=dram["l0f_lnb"], deps=bar),
        em.dma("q", "dX", out=b1c, in_=dram["l0f_b1"], deps=bar),
        em.dma("q", "dX", out=b2c, in_=dram["l0f_b2"], deps=bar),
        em.dma("q", "dX", out=hs_f, in_=dram["l0f_hs_f"], deps=bar),
        em.dma("q", "dX", out=hs_bf, in_=dram["l0f_hs_bf"], deps=bar),
        em.dma("q", "dX", out=selc[0:16, :], in_=dram["l0f_selc"], deps=bar),
        em.dma("q", "dX", out=b["win_add"], in_=dram["win_add"], deps=bar),
    ]
    ctok = [("dX", em.cnt["dX"]), ("dW", em.cnt["dW"])]

    for layer in range(2):
        ar.save()
        lsnap = em.snap()
        wqkv = ar.alloc(8 * 1536)
        wo = ar.alloc(4 * 1024)
        L0 = layer * 83968
        em.dma("q", "dQ", out=wqkv,
               in_=dram["scr_w0f"][:, L0 + W0F_OFF["wqkv"]:L0 + W0F_OFF["wqkv"] + 12288],
               deps=lsnap + ctx.w0f_cast)
        em.dma("q", "dQb", out=wo,
               in_=dram["scr_w0f"][:, L0 + W0F_OFF["wo"]:L0 + W0F_OFF["wo"] + 4096],
               deps=lsnap + ctx.w0f_cast)
        wtok = [("dQ", em.cnt["dQ"]), ("dQb", em.cnt["dQb"])]
        hb = Tb(ar.alloc(8 * 512), lsnap)
        qb = Tb(ar.alloc(4 * 512), lsnap)
        kb = [Tb(ar.alloc(4 * 512), lsnap), Tb(ar.alloc(4 * 512), lsnap)]
        vb = [Tb(ar.alloc(4 * 512), lsnap), Tb(ar.alloc(4 * 512), lsnap)]
        vT = [Tb(ar.alloc(8 * 4 * 64), lsnap), Tb(ar.alloc(8 * 4 * 64), lsnap)]
        A2 = [Tb(ar.alloc(640), lsnap), Tb(ar.alloc(640), lsnap)]
        T2 = [Tb(ar.alloc(640), lsnap), Tb(ar.alloc(640), lsnap)]
        ob = Tb(ar.alloc(4 * 512), lsnap)
        cmq = Tb(ar.alloc(512, F32), lsnap)
        smq = Tb(ar.alloc(512, F32), lsnap)
        cmk = Tb(ar.alloc(512, F32), lsnap)
        smk = Tb(ar.alloc(512, F32), lsnap)
        colsv = Tb(ar.alloc(8, F32), lsnap)
        gnr = Tb(ar.alloc(2 * 512, F32), lsnap)
        gnr_bf = Tb(ar.alloc(512), lsnap)
        xsq2 = ar.alloc(512, F32)
        N2 = [Tb(ar.alloc(16, F32), lsnap), Tb(ar.alloc(16, F32), lsnap)]
        O2 = [Tb(ar.alloc(64), lsnap), Tb(ar.alloc(64), lsnap)]

        for bband in range(NB):
            b0 = bband * 512
            bsnap = em.snap("t")
            hb_toks = []
            for ci in range(NCH):
                if ci < 4:
                    tk = em.on("s", lambda eng, ci=ci, b0=b0: eng.copy(
                        out=hb.ap[:, ci * 512:(ci + 1) * 512],
                        in_=x[:, xoff + ci * S + b0: xoff + ci * S + b0 + 512]),
                        deps=xx_toks + hb.wdeps() + bsnap)
                else:
                    if bband == 0:
                        tk0 = em.on("v", lambda eng, ci=ci: eng.memset(
                            hb.ap[:, ci * 512:ci * 512 + 1], 0.0),
                            deps=xx_toks + hb.wdeps() + bsnap)
                    else:
                        tk0 = em.on("v", lambda eng, ci=ci: eng.tensor_copy(
                            out=hb.ap[:, ci * 512:ci * 512 + 1],
                            in_=colsv.ap[:, ci:ci + 1]),
                            deps=[colsv.wtok] + hb.wdeps() + bsnap)
                    hb_toks.append(tk0)
                    colsv.read(tk0)
                    tk = em.on("s", lambda eng, ci=ci, b0=b0: eng.copy(
                        out=hb.ap[:, ci * 512 + 1:(ci + 1) * 512],
                        in_=x[:, xoff + ci * S + b0: xoff + ci * S + b0 + 511]),
                        deps=xx_toks + hb.wdeps() + bsnap)
                hb_toks.append(tk)
            hb.wrote(hb_toks[-1])
            t_cs = em.on("v", lambda eng, b0=b0: eng.tensor_copy(
                out=colsv.ap[:, 0:8].rearrange("p (a o) -> p a o", o=1),
                in_=x[:, xoff + b0 + 511:xoff + NCH * S:S].rearrange(
                    "p (a o) -> p a o", o=1)),
                deps=xx_toks + colsv.wdeps() + bsnap)
            colsv.wrote(t_cs)

            slot = bband % 2
            dsts = [(qb, 0), (qb, 1), (qb, 2), (qb, 3),
                    (kb[slot], 0), (kb[slot], 1), (kb[slot], 2), (kb[slot], 3),
                    (vb[slot], 0), (vb[slot], 1), (vb[slot], 2), (vb[slot], 3)]
            pj_toks = []

            def evac_qkv(m, bi, mmtok):
                buf, mc = dsts[m]
                d1 = em.on("s", lambda eng, buf=buf, mc=mc, bi=bi: eng.copy(
                    out=buf.ap[:, mc * 512:(mc + 1) * 512], in_=ps.f32(bi)),
                    deps=[mmtok] + (buf.wdeps() if mc == 0 else []))
                ps.read(bi, d1)
                buf.wrote(d1)
                pj_toks.append(d1)
            linear_groups(ctx, lambda k, m: wqkv[:, k * 1536 + m * 128:k * 1536 + (m + 1) * 128],
                          lambda k: hb.ap[:, k * 512:(k + 1) * 512],
                          NCH, list(range(12)), deps=hb_toks + wtok + [hb.wtok],
                          banks=[2, 3, 5], evac=evac_qkv)
            hb.read(pj_toks[-1])
            ctx.mark(f"b1.qkv.{bband}")

            def l2n(buf, scol):
                toks = []
                t_sq2 = None
                mmw = [None, None]
                for mc in range(4):
                    xb = b["xsq"] if mc % 2 == 0 else xsq2
                    t_q2 = em.on("s", lambda eng, buf=buf, mc=mc, xb=xb: eng.activation(
                        out=xb[:, 0:512], in_=buf.ap[:, mc * 512:(mc + 1) * 512],
                        func=AF.Square), deps=pj_toks + ([mmw[mc % 2]] if mmw[mc % 2] else []))
                    dep = [t_q2] + (ps.use(6) + ctok if mc == 0 else [])
                    t_sq2 = em.on("t", lambda eng, mc=mc, xb=xb: eng.matmul(
                        ps.f32(6)[0:16, 0:512], hs_f[:, mc * 16:(mc + 1) * 16],
                        xb[:, 0:512], start=(mc == 0), stop=(mc == 3)), deps=dep)
                    mmw[mc % 2] = t_sq2
                ps.set_wr(6, t_sq2)
                t_mx = em.on("v", lambda eng: eng.tensor_scalar_max(
                    gnr.ap[0:16, 0:512], ps.f32(6)[0:16, 0:512], 1e-24),
                    deps=[t_sq2] + gnr.wdeps())
                ps.read(6, t_mx)
                t_l = em.on("s", lambda eng: eng.activation(
                    out=gnr.ap[0:16, 512:1024], in_=gnr.ap[0:16, 0:512], func=AF.Ln),
                    deps=[t_mx])
                t_r = em.on("s", lambda eng: eng.activation(
                    out=gnr.ap[0:16, 0:512], in_=gnr.ap[0:16, 512:1024], func=AF.Exp,
                    scale=-0.5), deps=[t_l])
                gnr.wrote(t_r)
                t_bf = em.on("v", lambda eng: eng.tensor_copy(
                    out=gnr_bf.ap[0:16, 0:512], in_=gnr.ap[0:16, 0:512]),
                    deps=[t_r] + gnr_bf.wdeps())
                gnr_bf.wrote(t_bf)
                for mc in range(4):
                    bq = 6 if mc % 2 == 0 else 7
                    t_bc = em.on("t", lambda eng, mc=mc, bq=bq: eng.matmul(
                        ps.f32(bq), selc[0:16, mc * 128:(mc + 1) * 128],
                        gnr_bf.ap[0:16, 0:512], start=True, stop=True),
                        deps=[t_bf] + ps.use(bq))
                    ps.set_wr(bq, t_bc)
                    t_n = em.on("v", lambda eng, buf=buf, mc=mc, scol=scol, bq=bq:
                                eng.scalar_tensor_tensor(
                        out=buf.ap[:, mc * 512:(mc + 1) * 512],
                        in0=buf.ap[:, mc * 512:(mc + 1) * 512],
                        scalar=scol, in1=ps.f32(bq),
                        op0=ALU.mult, op1=ALU.mult), deps=[t_bc] + pj_toks + toks)
                    ps.read(bq, t_n)
                    buf.wrote(t_n)
                    toks.append(t_n)
                gnr_bf.read(toks[-1])
                return toks
            qn = l2n(qb, qscol[:, layer:layer + 1])
            kn = l2n(kb[slot], kscol[:, layer:layer + 1])

            mwd = cmq.wdeps() + smq.wdeps() + cmk.wdeps() + smk.wdeps()
            em.dma("q", "dX", out=cmq.ap[:, :], in_=dram["l0f_cq"][:, b0:b0 + 512],
                   deps=mwd + ctok)
            em.dma("q", "dX", out=smq.ap[:, :], in_=dram["l0f_sq"][:, b0:b0 + 512], deps=())
            em.dma("q", "dX", out=cmk.ap[:, :], in_=dram["l0f_ck"][:, b0:b0 + 512], deps=())
            em.dma("q", "dX", out=smk.ap[:, :], in_=dram["l0f_sk"][:, b0:b0 + 512], deps=())
            t_mg = ("dX", em.cnt["dX"])
            ropes = []
            for mc in range(4):
                ropes.append((qb, mc, cmq, smq, qn))
                ropes.append((kb[slot], mc, cmk, smk, kn))
            rp_toks = []
            for i2, (bufx, mc, mpc, mps, dd) in enumerate(ropes):
                bank = 6 + (i2 % 2)
                sA, sB = (spA, spB) if i2 % 2 == 0 else (spC, spD)
                t_rr = emit_rope(ctx, bufx.ap[:, mc * 512:(mc + 1) * 512], bank,
                                 mpc.ap, mps.ap, sA, sB, deps=dd, map_deps=[t_mg])
                rp_toks.append(t_rr)
                bufx.wrote(t_rr)
            for mm_ in (cmq, smq, cmk, smk):
                mm_.wrote(t_mg)
                mm_.read(rp_toks[-1])
            ctx.mark(f"b1.rope.{bband}")

            vt_toks = []
            for mc in range(4):
                tr_last = None
                for z in range(4):
                    for hh in range(2):
                        bh = hh * 64
                        cslot = (z * 2 + hh) * 64
                        tr_last = em.on("t", lambda eng, mc=mc, z=z, bh=bh, slot=slot,
                                        cslot=cslot: eng.transpose(
                            ps.bf16(7)[:, cslot:cslot + 64],
                            vb[slot].ap[bh:bh + 64, mc * 512 + z * 128:mc * 512 + (z + 1) * 128],
                            b["ident"][bh:bh + 64, bh:bh + 64]),
                            deps=pj_toks + ps.use(7))
                        ps.set_wr(7, tr_last)
                        vb[slot].read(tr_last)
                t_ve = em.on("v", lambda eng, mc=mc, slot=slot: eng.tensor_copy(
                    out=vT[slot].ap[:, mc * 512:(mc + 1) * 512],
                    in_=ps.bf16(7)[:, 0:512]),
                    deps=[tr_last] + (vT[slot].wdeps() if mc == 0 else []))
                ps.read(7, t_ve)
                vt_toks.append(t_ve)
            vT[slot].wrote(vt_toks[-1])
            ctx.mark(f"b1.vt.{bband}")

            ao_deps = rp_toks + vt_toks
            iters = [(qt, h) for qt in range(4) for h in range(8)]
            pend = {}

            def stage1a(i, bband=bband, qb=qb, kb=kb, iters=iters,
                        pend=pend, ao_deps=ao_deps):
                qt, h = iters[i]
                p = i % 2
                base = p * 1024
                gq = bband * 4 + qt
                k_lo = max(0, gq - 4)
                nkc = gq - k_lo + 1
                nk = nkc * 128
                bh = (h % 2) * 64
                mc = h // 2
                q_ap = qb.ap[bh:bh + 64, mc * 512 + qt * 128:mc * 512 + (qt + 1) * 128]
                tok = None
                for j in range(nkc):
                    kc = k_lo + j
                    kslot = (kc // 4) % 2
                    koff = (kc % 4) * 128
                    dep = (ao_deps + ps.use(2 * p) + ps.use(2 * p + 1)) if j == 0 else ()
                    tok = em.on("t", lambda eng, j=j, kslot=kslot, koff=koff,
                                q_ap=q_ap, mc=mc, bh=bh, base=base, kb=kb: eng.matmul(
                        ps.tens[0][:, base + j * 128:base + (j + 1) * 128], q_ap,
                        kb[kslot].ap[bh:bh + 64, mc * 512 + koff:mc * 512 + koff + 128],
                        start=True, stop=True),
                        deps=dep, noinc=(j < nkc - 1))
                ps.set_wr(2 * p, tok)
                ps.set_wr(2 * p + 1, tok)
                pend[i] = [tok, gq, nk, nkc, k_lo, mc, bh]

            def stage1b(i, A2=A2, N2=N2, pend=pend):
                tok, gq, nk, nkc, k_lo, mc, bh = pend[i]
                p = i % 2
                base = p * 1024
                t_mk = em.on("v", lambda eng, nk=nk, base=base: eng.tensor_add(
                    ps.tens[0][:, base + nk - 128:base + nk],
                    ps.tens[0][:, base + nk - 128:base + nk],
                    b["causal_add"]), deps=[tok])
                mask_tok = t_mk
                if gq >= 4:
                    mask_tok = em.on("v", lambda eng, base=base: eng.tensor_add(
                        ps.tens[0][:, base:base + 128], ps.tens[0][:, base:base + 128],
                        b["win_add"]), deps=[t_mk] + ctok)
                ps.set_wr(2 * p, mask_tok)
                ps.set_wr(2 * p + 1, mask_tok)
                nx = N2[p]
                ab = A2[p]
                t_ex = em.on("s", lambda eng, nk=nk, base=base, nx=nx, ab=ab: eng.activation(
                    out=ab.ap[:, 0:nk], in_=ps.tens[0][:, base:base + nk], func=AF.Exp,
                    scale=8.0, accum_out=nx.ap[:, 2:3]),
                    deps=[mask_tok] + ab.wdeps() + nx.wdeps())
                ab.wrote(t_ex)
                nx.wrote(t_ex)
                ps.read(2 * p, t_ex)
                ps.read(2 * p + 1, t_ex)
                pend[i] = [t_ex, nkc, k_lo, mc, bh]

            def stage2a(i, A2=A2, pend=pend):
                t_ex, nkc = pend[i][0], pend[i][1]
                p = i % 2
                ab = A2[p]
                tr_last = None
                for j in range(nkc):
                    dtr = ([t_ex] + ps.use(6 + p)) if j == 0 else [t_ex]
                    tr_last = em.on("t", lambda eng, j=j, p=p, ab=ab: eng.transpose(
                        ps.bf16(6 + p)[:, j * 128:(j + 1) * 128],
                        ab.ap[:, j * 128:(j + 1) * 128], b["ident"]),
                        deps=dtr)
                    ab.read(tr_last)
                ps.set_wr(6 + p, tr_last)
                pend[i].append(tr_last)

            def stage2b(i, T2=T2, N2=N2, O2=O2, vT=vT, pend=pend,
                        vt_toks=vt_toks):
                t_ex, nkc, k_lo, mc, bh, tr_last = pend[i][:6]
                p = i % 2
                at, nx, oq = T2[p], N2[p], O2[p]
                t_ae = em.on("v", lambda eng, nkc=nkc, p=p, at=at: eng.tensor_copy(
                    out=at.ap[:, 0:nkc * 128],
                    in_=ps.bf16(6 + p)[:, 0:nkc * 128]),
                    deps=[tr_last] + at.wdeps())
                ps.read(6 + p, t_ae)
                at.wrote(t_ae)
                tok2 = None
                for j in range(nkc):
                    kc = k_lo + j
                    kslot = (kc // 4) % 2
                    z = kc % 4
                    vcol = mc * 512 + (z * 2 + (bh // 64)) * 64
                    dep = ([t_ae] + vt_toks + ps.use(4)) if j == 0 else ()
                    tok2 = em.on("t", lambda eng, j=j, kslot=kslot, vcol=vcol,
                                 nkc=nkc, at=at, vT=vT: eng.matmul(
                        ps.f32(4)[:, 0:64],
                        at.ap[:, j * 128:(j + 1) * 128],
                        vT[kslot].ap[:, vcol:vcol + 64],
                        start=(j == 0), stop=(j == nkc - 1)),
                        deps=dep, noinc=(j != nkc - 1))
                ps.set_wr(4, tok2)
                at.read(tok2)
                vT[0].read(tok2)
                vT[1].read(tok2)
                t_rl = em.on("v", lambda eng, nx=nx: eng.reciprocal(
                    nx.ap[:, 3:4], nx.ap[:, 2:3]), deps=[t_ex])
                nx.wrote(t_rl)
                t_dv = em.on("v", lambda eng, nx=nx, oq=oq: eng.tensor_scalar(
                    out=oq.ap[:, 0:64], in0=ps.f32(4)[:, 0:64],
                    scalar1=nx.ap[:, 3:4], scalar2=None, op0=ALU.mult),
                    deps=[tok2, t_rl] + oq.wdeps())
                oq.wrote(t_dv)
                nx.read(t_dv)
                ps.read(4, t_dv)
                pend[i].append(t_dv)

            def stage3(i, O2=O2, ob=ob, iters=iters, pend=pend):
                qt, h = iters[i]
                p = i % 2
                t_dv = pend[i][-1]
                bh = (h % 2) * 64
                mc = h // 2
                oq = O2[p]
                last_ot = em.on("t", lambda eng, bh=bh, mc=mc, oq=oq: eng.transpose(
                    ps.bf16(5)[bh:bh + 64, mc * 128:(mc + 1) * 128],
                    oq.ap[:, 0:64], b["ident"]),
                    deps=[t_dv] + (ps.use(5) if h == 0 else []))
                oq.read(last_ot)
                ps.set_wr(5, last_ot)
                if h == 7:
                    t_oe = em.on("v", lambda eng, qt=qt, ob=ob: eng.tensor_copy(
                        out=ob.ap.rearrange("p (cc tt) -> p cc tt", cc=4)
                        [:, :, qt * 128:(qt + 1) * 128],
                        in_=ps.bf16(5)[:, 0:512].rearrange("p (cc tt) -> p cc tt", cc=4)),
                        deps=[last_ot] + (ob.wdeps() if qt == 0 else []))
                    ps.read(5, t_oe)
                    ob.wrote(t_oe)

            n_it = len(iters)
            stage1a(0)
            if n_it > 1:
                stage1a(1)
            for i in range(n_it + 2):
                if 2 <= i + 1 < n_it:
                    stage1a(i + 1)
                if 1 <= i <= n_it:
                    stage2a(i - 1)
                if i < n_it:
                    stage1b(i)
                if 1 <= i <= n_it:
                    stage2b(i - 1)
                if i >= 2:
                    stage3(i - 2)
            ctx.mark(f"b1.attn.{bband}")

            def evac_wo(m, bi, mmtok, b0=b0):
                x_ap = x[:, xoff + m * S + b0: xoff + m * S + b0 + 512]
                tk = em.on("v", lambda eng, x_ap=x_ap, bi=bi: eng.tensor_add(
                    x_ap, ps.f32(bi), x_ap), deps=[mmtok, colsv.wtok, hb.wtok] + pj_toks)
                ps.read(bi, tk)
            linear_groups(ctx, lambda k, m: wo[:, k * 1024 + m * 128:k * 1024 + (m + 1) * 128],
                          lambda k: ob.ap[:, k * 512:(k + 1) * 512],
                          NCH // 2, list(range(NCH)), deps=[ob.wtok], banks=[2, 3],
                          evac=evac_wo)
        ar.restore()
        ctx.mark(f"l0f.b1.{layer}")

        ar.save()
        b2snap = em.snap()
        hbt = [Tb(ar.alloc(8 * 512), b2snap), Tb(ar.alloc(8 * 512), b2snap)]
        xbc = Tb(ar.alloc(8 * NT, F32), b2snap)
        hsub = Tb(ar.alloc(11 * 512), b2snap)
        glb2 = [Tb(ar.alloc(512), b2snap), Tb(ar.alloc(512), b2snap)]
        Ccol = Tb(ar.alloc(2 * 22, F32), b2snap)
        w1 = ar.alloc(8 * 2816)
        w2 = ar.alloc(11 * 1024)
        t_xb = em.on("v", lambda eng: eng.tensor_copy(
            out=xbc.ap[:, 0:8 * NT].rearrange("p (tt a) -> p tt a", tt=NT),
            in_=x[:, xoff:xoff + NCH * S].rearrange("p (a z) -> p a z", a=NCH)
            [:, :, 511::512].transpose([0, 2, 1])),
            deps=b2snap + xbc.wdeps())
        xbc.wrote(t_xb)

        mu = b["rows_f32"][0:1, 0:S]
        ex2 = b["rows_f32"][0:1, 2048:2048 + S]
        r_f = b["rows_f32"][0:1, 4096:4096 + S]
        r_bf = b["rows_bf"][0:1, 0:S]
        mu_bf = b["rows_f32"][0:1, 4096:6144].bitcast(BF16)[0:1, 0:S]
        sq_war = [None]
        rf_tok = {}
        for t in range(NT):
            hh = hbt[t % 2]
            toks = []
            for ci in range(NCH):
                if ci < 4:
                    tk = em.on("s", lambda eng, ci=ci, t=t, hh=hh: eng.copy(
                        out=hh.ap[:, ci * 512:(ci + 1) * 512],
                        in_=x[:, xoff + ci * S + t * 512: xoff + ci * S + (t + 1) * 512]),
                        deps=b2snap + hh.wdeps())
                else:
                    if t == 0:
                        tk0 = em.on("v", lambda eng, ci=ci, hh=hh: eng.memset(
                            hh.ap[:, ci * 512:ci * 512 + 1], 0.0),
                            deps=b2snap + hh.wdeps())
                    else:
                        tk0 = em.on("v", lambda eng, ci=ci, t=t, hh=hh: eng.tensor_copy(
                            out=hh.ap[:, ci * 512:ci * 512 + 1],
                            in_=xbc.ap[:, (t - 1) * 8 + ci:(t - 1) * 8 + ci + 1]),
                            deps=[xbc.wtok] + hh.wdeps())
                    toks.append(tk0)
                    tk = em.on("s", lambda eng, ci=ci, t=t, hh=hh: eng.copy(
                        out=hh.ap[:, ci * 512 + 1:(ci + 1) * 512],
                        in_=x[:, xoff + ci * S + t * 512: xoff + ci * S + t * 512 + 511]),
                        deps=b2snap + hh.wdeps())
                toks.append(tk)
            hh.wrote(toks[-1])
            tok_sum = None
            for ci in range(NCH):
                dep = (toks + ps.use(6)) if ci == 0 else ()
                tok_sum = em.on("t", lambda eng, ci=ci, hh=hh: eng.matmul(
                    ps.f32(6)[0:1, :], b["ones_col_bf"][:, 0:1],
                    hh.ap[:, ci * 512:(ci + 1) * 512],
                    start=(ci == 0), stop=(ci == NCH - 1)),
                    deps=dep, noinc=(ci != NCH - 1))
            ps.set_wr(6, tok_sum)
            hh.read(tok_sum)
            tok_sq = None
            tksq = None
            for ci in range(NCH):
                sdep = toks + ([sq_war[0]] if sq_war[0] else [])
                tksq = em.on("s", lambda eng, ci=ci, hh=hh: eng.activation(
                    out=b["xsq"][:, 0:512], in_=hh.ap[:, ci * 512:(ci + 1) * 512],
                    func=AF.Square), deps=sdep)
                mdep = [tksq] + (ps.use(7) if ci == 0 else [])
                tok_sq = em.on("t", lambda eng, ci=ci: eng.matmul(
                    ps.f32(7)[0:1, :], b["ones_col"][:, 0:1], b["xsq"][:, 0:512],
                    start=(ci == 0), stop=(ci == NCH - 1)), deps=mdep)
                sq_war[0] = tok_sq
            ps.set_wr(7, tok_sq)
            hh.read(tksq)
            m_ap = mu[:, t * 512:(t + 1) * 512]
            e_ap = ex2[:, t * 512:(t + 1) * 512]
            f_ap = r_f[:, t * 512:(t + 1) * 512]
            tk1 = em.on("v", lambda eng, m_ap=m_ap: eng.tensor_scalar_mul(
                m_ap, ps.f32(6)[0:1, :], 1.0 / D), deps=[tok_sum])
            ps.read(6, tk1)
            tk2 = em.on("v", lambda eng, e_ap=e_ap: eng.tensor_scalar_mul(
                e_ap, ps.f32(7)[0:1, :], 1.0 / D), deps=[tok_sq])
            ps.read(7, tk2)
            t_msq = em.on("v", lambda eng, m_ap=m_ap, f_ap=f_ap: eng.tensor_mul(
                f_ap, m_ap, m_ap), deps=[tk1])
            t_var = em.on("v", lambda eng, e_ap=e_ap, f_ap=f_ap: eng.tensor_sub(
                e_ap, e_ap, f_ap), deps=[t_msq, tk2])
            t_ln = em.on("s", lambda eng, e_ap=e_ap, f_ap=f_ap: eng.activation(
                out=f_ap, in_=e_ap, func=AF.Ln, bias=EPS), deps=[t_var])
            t_r = em.on("s", lambda eng, e_ap=e_ap, f_ap=f_ap: eng.activation(
                out=e_ap, in_=f_ap, func=AF.Exp, scale=-0.5), deps=[t_ln])
            rf_tok[t] = t_r
            t_rbf = em.on("v", lambda eng, e_ap=e_ap, t=t: eng.tensor_scalar_mul(
                r_bf[:, t * 512:(t + 1) * 512], e_ap, 1.0), deps=[t_r])
            t_mubf = em.on("v", lambda eng, m_ap=m_ap, t=t: eng.tensor_scalar_mul(
                mu_bf[:, t * 512:(t + 1) * 512], m_ap, 1.0),
                deps=[t_rbf, rf_tok[t // 2]])
            st_t = {"r_bf": r_bf, "mu_bf": mu_bf, "toks": [t_rbf, t_mubf]}
            xn_toks = ln_normalize_tile(
                ctx, st_t, lambda ci, tt, hh=hh: hh.ap[:, ci * 512:(ci + 1) * 512],
                dep_src=[tok_sum, tksq], t=t,
                xn_dst=lambda ci, hh=hh: hh.ap[:, ci * 512:(ci + 1) * 512], sc=spA,
                gamma_col=lngc[:, layer * 8:(layer + 1) * 8], bank_r=6, bank_m=7)
            hh.wrote(xn_toks[-1])
            t_sp = em.dma("q", "dS" if t % 2 == 0 else "dSb",
                          out=dram["scr_xn"][:, t * 4096:(t + 1) * 4096],
                          in_=hh.ap[:, 0:4096], deps=xn_toks)
            hh.read(t_sp)
        hb2_done = [("dS", em.cnt["dS"]), ("dSb", em.cnt["dSb"])]
        ctx.mark(f"l0f.pp.{layer}")

        w1_rd = []
        w2_rd = []
        for sub in range(2):
            Lw = layer * 83968
            t_w1 = em.dma("q", "dQ",
                          out=w1.rearrange("p (c n) -> p c n", c=8),
                          in_=dram["scr_w0f"][:, Lw:Lw + 45056]
                          .rearrange("p (c n) -> p c n", c=8)
                          [:, :, sub * 2816:(sub + 1) * 2816],
                          deps=b2snap + ctx.w0f_cast + w1_rd)
            wtk1 = [("dQ", em.cnt["dQ"])]
            t_w2 = em.dma("q", "dQb", out=w2,
                          in_=dram["scr_w0f"][:, Lw + 45056 + sub * 11264:
                                              Lw + 45056 + (sub + 1) * 11264],
                          deps=b2snap + ctx.w0f_cast + w2_rd)
            wtk2 = [("dQb", em.cnt["dQb"])]
            t_cc = None
            for mj in range(22):
                for kc in range(NCH):
                    dep = (wtk1 + ctok + ps.use(6)) if (mj == 0 and kc == 0) else ()
                    t_cc = em.on("t", lambda eng, mj=mj, kc=kc, layer=layer: eng.matmul(
                        ps.f32(6)[:, mj:mj + 1],
                        w1[:, kc * 2816 + mj * 128:kc * 2816 + (mj + 1) * 128],
                        lnbc[:, layer * 8 + kc:layer * 8 + kc + 1],
                        start=(kc == 0), stop=(kc == NCH - 1)),
                        deps=dep, noinc=not (mj == 21 and kc == NCH - 1))
            ps.set_wr(6, t_cc)
            t_cv = em.on("v", lambda eng, sub=sub, layer=layer: eng.scalar_tensor_tensor(
                out=Ccol.ap[:, 0:22], in0=ps.f32(6)[:, 0:22], scalar=1.0,
                in1=b1c[:, layer * 44 + sub * 22:layer * 44 + (sub + 1) * 22],
                op0=ALU.mult, op1=ALU.add), deps=[t_cc] + Ccol.wdeps())
            Ccol.wrote(t_cv)
            ps.read(6, t_cv)

            for t in range(NT):
                gt = sub * NT + t
                par = gt % 2
                hh = hbt[par]
                xsem = "dX" if par == 0 else "dXb"
                t_ld = em.dma("q", xsem, out=hh.ap[:, 0:4096],
                              in_=dram["scr_xn"][:, t * 4096:(t + 1) * 4096],
                              deps=hh.wdeps() + hb2_done)
                hh.wrote(t_ld)
                ldk = [(xsem, em.cnt[xsem])]
                h_toks = []
                for mj in range(11):
                    ba, bg = (0, 1) if mj % 2 == 0 else (4, 5)
                    toka = None
                    for k in range(NCH):
                        dep = (wtk1 + ldk + ps.use(ba)) if k == 0 else ()
                        toka = em.on("t", lambda eng, k=k, mj=mj, ba=ba, hh=hh: eng.matmul(
                            ps.f32(ba), w1[:, k * 2816 + mj * 128:k * 2816 + (mj + 1) * 128],
                            hh.ap[:, k * 512:(k + 1) * 512],
                            start=(k == 0), stop=(k == NCH - 1)),
                            deps=dep, noinc=(k != NCH - 1))
                    ps.set_wr(ba, toka)
                    tokg = None
                    for k in range(NCH):
                        dep = (wtk1 + ldk + ps.use(bg)) if k == 0 else ()
                        tokg = em.on("t", lambda eng, k=k, mj=mj, bg=bg, hh=hh: eng.matmul(
                            ps.f32(bg), w1[:, k * 2816 + (11 + mj) * 128:k * 2816 + (12 + mj) * 128],
                            hh.ap[:, k * 512:(k + 1) * 512],
                            start=(k == 0), stop=(k == NCH - 1)),
                            deps=dep, noinc=(k != NCH - 1))
                    ps.set_wr(bg, tokg)
                    if mj == 10:
                        hh.read(tokg)
                        w1_rd = [tokg]
                    gl = glb2[mj % 2]
                    t_gl = em.on("s", lambda eng, mj=mj, bg=bg, gl=gl: eng.activation(
                        out=gl.ap[:, 0:512], in_=ps.f32(bg), func=AF.Gelu,
                        bias=Ccol.ap[:, 11 + mj:12 + mj]),
                        deps=[tokg, t_cv] + gl.wdeps())
                    gl.wrote(t_gl)
                    ps.read(bg, t_gl)
                    t_h = em.on("v", lambda eng, mj=mj, ba=ba, gl=gl: eng.scalar_tensor_tensor(
                        out=hsub.ap[:, mj * 512:(mj + 1) * 512], in0=ps.f32(ba),
                        scalar=Ccol.ap[:, mj:mj + 1], in1=gl.ap[:, 0:512],
                        op0=ALU.add, op1=ALU.mult),
                        deps=[toka, t_gl, t_cv] + (hsub.wdeps() if mj == 0 else []))
                    gl.read(t_h)
                    Ccol.read(t_h)
                    ps.read(ba, t_h)
                    hsub.wrote(t_h)
                    h_toks.append(t_h)
                for m in range(NCH):
                    bi = 2 + (m % 2)
                    tok3 = None
                    for kj in range(11):
                        dep = (h_toks + wtk2 + ps.use(bi)) if kj == 0 else ()
                        tok3 = em.on("t", lambda eng, kj=kj, m=m, bi=bi: eng.matmul(
                            ps.f32(bi), w2[:, kj * 1024 + m * 128:kj * 1024 + (m + 1) * 128],
                            hsub.ap[:, kj * 512:(kj + 1) * 512],
                            start=(kj == 0), stop=(kj == 10)),
                            deps=dep, noinc=(kj != 10))
                    ps.set_wr(bi, tok3)
                    hsub.read(tok3)
                    if m == NCH - 1:
                        w2_rd = [tok3]
                    x_ap = x[:, xoff + m * S + t * 512: xoff + m * S + (t + 1) * 512]
                    if sub == 0:
                        tk = em.on("v", lambda eng, x_ap=x_ap, bi=bi, m=m, layer=layer:
                                   eng.scalar_tensor_tensor(
                                       out=x_ap, in0=ps.f32(bi),
                                       scalar=b2c[:, layer * 8 + m:layer * 8 + m + 1],
                                       in1=x_ap, op0=ALU.add, op1=ALU.add),
                                   deps=[tok3] + hb2_done + ctok)
                    else:
                        tk = em.on("v", lambda eng, x_ap=x_ap, bi=bi: eng.tensor_add(
                            x_ap, x_ap, ps.f32(bi)), deps=[tok3] + hb2_done)
                    ps.read(bi, tk)
            ctx.mark(f"l0f.s{sub}.{layer}")
        ar.restore()

    fin = em.snap()
    xrb = Tb(ar.alloc(8 * 512, F32), fin)
    for t in range(NT):
        t_lx = em.dma("q", "dX", out=xrb.ap[:, 0:4096],
                      in_=dram["scr_x"].rearrange("p (c z) -> p c z", c=8)
                      [:, :, t * 512:(t + 1) * 512],
                      deps=xrb.wdeps() + fin + spx)
        xrb.wrote(t_lx)
        lxk = [("dX", em.cnt["dX"])]
        for ci in range(NCH):
            tk = em.on("v", lambda eng, ci=ci, t=t: eng.tensor_add(
                xs(ci, t), xs(ci, t), xrb.ap[:, ci * 512:(ci + 1) * 512]),
                deps=lxk + fin)
            xrb.read(tk)
    ar.restore()


SUB_BUILDERS["l0f"] = sub_l0f



def emit_quant(ctx, src_ap, dst_ap, qs, deps):
    em, b = ctx.em, ctx.b
    F = src_ap.shape[1]
    G = F // 16
    gm, t, hi, lo, mk = qs["gm"], qs["t"], qs["hi"], qs["lo"], qs["mk"]
    ic = b["iconsts"]
    t0 = em.on("v", lambda eng: eng.tensor_scalar(
        out=t.ap[:, 0:F].bitcast(mybir.dt.int32),
        in0=src_ap.bitcast(mybir.dt.int32),
        scalar1=ic[:, 2:3], scalar2=None, op0=ALU.bitwise_and),
        deps=list(deps) + t.wdeps())
    t.wrote(t0)
    t1 = em.on("v", lambda eng: eng.tensor_reduce(
        gm.ap[:, 0:G].rearrange("p (g o) -> p g o", o=1),
        t.ap[:, 0:F].rearrange("p (g e) -> p g e", e=16),
        axis=AX.X, op=ALU.max), deps=[t0] + gm.wdeps())
    t.read(t1)
    gm.wrote(t1)
    t2 = em.on("v", lambda eng: eng.tensor_scalar(
        out=gm.ap[:, G:2 * G], in0=gm.ap[:, 0:G], scalar1=1.0 / 6.0,
        scalar2=1e-12, op0=ALU.mult, op1=ALU.max), deps=[t1])
    gm.wrote(t2)
    t3 = em.on("v", lambda eng: eng.reciprocal(gm.ap[:, 2 * G:3 * G],
                                               gm.ap[:, G:2 * G]), deps=[t2])
    gm.wrote(t3)
    t4 = em.on("v", lambda eng: eng.tensor_tensor(
        out=t.ap[:, 0:F].rearrange("p (g e) -> p g e", e=16),
        in0=src_ap.rearrange("p (g e) -> p g e", e=16),
        in1=gm.ap[:, 2 * G:3 * G].rearrange("p (g o) -> p g o", o=1)
        .broadcast_to([128, G, 16]), op=ALU.mult),
        deps=[t3, t1] + list(deps) + t.wdeps())
    t.wrote(t4)
    t5 = em.on("v", lambda eng: eng.tensor_tensor(
        out=hi.ap[:, 0:F].bitcast(mybir.dt.int32), in0=t.ap[:, 0:F].bitcast(mybir.dt.int32),
        in1=ic[:, 0:1].broadcast_to([128, F]), op=ALU.add),
        deps=[t4] + hi.wdeps())
    hi.wrote(t5)
    t6 = em.on("v", lambda eng: eng.tensor_scalar(
        out=hi.ap[:, 0:F].bitcast(mybir.dt.int32), in0=hi.ap[:, 0:F].bitcast(mybir.dt.int32),
        scalar1=ic[:, 1:2], scalar2=None, op0=ALU.bitwise_and), deps=[t5])
    hi.wrote(t6)
    t7 = em.on("v", lambda eng: eng.tensor_scalar(
        out=hi.ap[:, 0:F], in0=hi.ap[:, 0:F], scalar1=-6.0, scalar2=6.0,
        op0=ALU.max, op1=ALU.min), deps=[t6])
    hi.wrote(t7)
    t8 = em.on("s", lambda eng: eng.activation(
        out=lo.ap[:, 0:F], in_=t.ap[:, 0:F], func=AF.Identity,
        scale=2.0, bias=12582912.0), deps=[t4] + lo.wdeps())
    lo.wrote(t8)
    t9 = em.on("s", lambda eng: eng.activation(
        out=lo.ap[:, 0:F], in_=lo.ap[:, 0:F], func=AF.Identity,
        scale=0.5, bias=-6291456.0), deps=[t8])
    lo.wrote(t9)
    ta = em.on("v", lambda eng: eng.tensor_scalar(
        out=mk.ap[:, 0:F].bitcast(mybir.dt.int32), in0=t.ap[:, 0:F].bitcast(mybir.dt.int32),
        scalar1=ic[:, 2:3], scalar2=None, op0=ALU.bitwise_and),
        deps=[t4] + mk.wdeps())
    mk.wrote(ta)
    tb = em.on("v", lambda eng: eng.tensor_single_scalar(
        out=mk.ap[:, 0:F].bitcast(mybir.dt.int32), in_=mk.ap[:, 0:F], scalar=1.0,
        op=ALU.is_gt), deps=[ta])
    mk.wrote(tb)
    tc = em.on("v", lambda eng: eng.select(
        out=t.ap[:, 0:F], mask=mk.ap[:, 0:F].bitcast(mybir.dt.int32),
        on_true=hi.ap[:, 0:F],
        on_false=lo.ap[:, 0:F], add_drain=True), deps=[tb, t7, t9, t4])
    t.wrote(tc)
    mk.read(tc)
    hi.read(tc)
    lo.read(tc)
    td = em.on("v", lambda eng: eng.tensor_tensor(
        out=dst_ap.rearrange("p (g e) -> p g e", e=16),
        in0=t.ap[:, 0:F].rearrange("p (g e) -> p g e", e=16),
        in1=gm.ap[:, G:2 * G].rearrange("p (g o) -> p g o", o=1)
        .broadcast_to([128, G, 16]), op=ALU.mult), deps=[tc, t2])
    t.read(td)
    gm.read(td)
    return td







def build_program(S, n_rows, subs, dram_specs):
    nc = bass.Bass("TRN2", target_bir_lowering=False)
    dram = {}
    for nm, (shape, dt) in dram_specs.items():
        kind = "ExternalOutput" if nm == "y" else "ExternalInput"
        dram[nm] = nc.dram_tensor(nm, list(shape), dt, kind=kind).ap()
    dram["scr_xn"] = nc.dram_tensor("scr_xn", [128, NCH * S], BF16, kind="Internal").ap()
    dram["scr_out"] = nc.dram_tensor("scr_out", [128, NCH * S], BF16, kind="Internal").ap()
    dram["scr_wq"] = nc.dram_tensor("scr_wq", [128, 3 * 8 * 2816], BF16, kind="Internal").ap()
    dram["scr_x"] = nc.dram_tensor("scr_x", [128, NCH * S], F32, kind="Internal").ap()
    dram["scr_w0f"] = nc.dram_tensor("scr_w0f", [128, 2 * 83968], BF16, kind="Internal").ap()

    stack = contextlib.ExitStack()
    with stack:
        em = Em(nc, stack)
        ps = Banks(nc, stack)
        ar = Arena(nc, stack)
        b = {}
        b["x_res"] = stack.enter_context(nc.sbuf_tensor("x_res", [128, NCH * S], F32))[:, :]
        b["rows_f32"] = stack.enter_context(nc.sbuf_tensor("rows_f32", [1, 6144], F32))[:, :]
        b["rows_bf"] = stack.enter_context(nc.sbuf_tensor("rows_bf", [1, 4096], BF16))[:, :]
        b["xsq"] = stack.enter_context(nc.sbuf_tensor("xsq", [128, 512], F32))[:, :]
        b["ident"] = stack.enter_context(nc.sbuf_tensor("ident_s", [128, 128], BF16))[:, :]
        b["ones_col"] = stack.enter_context(nc.sbuf_tensor("ones_col_s", [128, 1], F32))[:, :]
        b["ones_row_bf"] = stack.enter_context(nc.sbuf_tensor("ones_row_bf_s", [1, 128], BF16))[:, :]
        b["triu_bf"] = stack.enter_context(nc.sbuf_tensor("triu_bf", [128, 128], BF16))[:, :]
        b["fconsts"] = stack.enter_context(
            nc.sbuf_tensor("fconsts_s", [128, len(FCONST_VALS)], F32))[:, :]
        b["rope_swap"] = stack.enter_context(
            nc.sbuf_tensor("rope_swap_s", [128, 128], BF16))[:, :]
        b["causal_add"] = stack.enter_context(
            nc.sbuf_tensor("causal_add_s", [128, 128], BF16))[:, :]
        b["iconsts"] = stack.enter_context(
            nc.sbuf_tensor("iconsts_s", [128, 3], mybir.dt.int32))[:, :]
        b["ident_f"] = stack.enter_context(
            nc.sbuf_tensor("ident_f_s", [128, 128], F32))[:, :]
        b["win_add"] = stack.enter_context(
            nc.sbuf_tensor("win_add_s", [128, 128], BF16))[:, :]
        b["ones_col_bf"] = stack.enter_context(
            nc.sbuf_tensor("ones_col_bf_s", [128, 1], BF16))[:, :]
        for j, vv in enumerate(FCONST_VALS):
            nc.const_aps.register(F32, vv, b["fconsts"][:, j:j + 1])

        ctx = Ctx(nc, em, ps, ar, S, b)
        lamh = 1.0 - 2.0 ** (-5.0 - np.arange(16, dtype=np.float64))
        ctx.lam_pows = [float(lamh[h] ** 128) for h in range(16)]

        ctoks = [
            em.dma("q", "dX", out=b["ident"], in_=dram["ident"]),
            em.dma("q", "dX", out=b["ones_col"], in_=dram["ones_col"]),
            em.dma("q", "dX", out=b["ones_row_bf"], in_=dram["ones_row_bf"]),
            em.dma("q", "dX", out=b["triu_bf"], in_=dram["triu_incl_bf"]),
            em.dma("q", "dX", out=b["fconsts"], in_=dram["fconsts"]),
            em.dma("q", "dX", out=b["rope_swap"], in_=dram["rope_swap"]),
            em.dma("q", "dX", out=b["causal_add"], in_=dram["causal_add"]),
            em.dma("q", "dX", out=b["iconsts"], in_=dram["iconsts"]),
            em.dma("q", "dX", out=b["ident_f"], in_=dram["ident_f"]),
            em.dma("q", "dX", out=b["ones_col_bf"], in_=dram["ones_col_bf"]),
        ]

        ctx.w0f_cast = []
        if any(sub == "l0f" for sub in subs):
            emit_w0f_cast(ctx, dram)
        if any(sub == "l1f" for sub in subs):
            emit_wquant_l1f(ctx, dram)
        for row in range(n_rows):
            t_x = em.dma("q", "dX",
                         out=b["x_res"].rearrange("p (c t) -> p c t", c=NCH),
                         in_=dram["x"][row].rearrange("(c p) t -> p c t", p=128),
                         deps=em.snap() + ctoks + ctx.w0f_cast)
            for sub in subs:
                SUB_BUILDERS[sub](ctx, dram, 0)
                ctx.mark(sub)
            t_y = em.dma("q", "dO",
                         out=dram["y"][row].rearrange("(c p) t -> p c t", p=128),
                         in_=b["x_res"].rearrange("p (c t) -> p c t", c=NCH),
                         deps=em.snap())
            em.on("q", lambda eng: eng.nop(), deps=[t_y], noinc=True)
        em.finalize()
    return nc



def _colform(v):
    n = v.shape[-1]
    return np.ascontiguousarray(v.reshape(n // 128, 128).T).astype(np.float32)


FCONST_VALS = [1e-5, 1.0, -1.0, -0.5, 0.5, 2.0, -2.0, 6.0, 0.75, 8.0, 1e-12, 12582912.0, -6291456.0]


def host_constants():
    import ml_dtypes
    i = np.arange(128)
    fc = np.zeros((128, len(FCONST_VALS)), np.float32)
    for j, vv in enumerate(FCONST_VALS):
        fc[:, j] = vv
    sw = np.zeros((128, 128), np.float32)
    for p in range(128):
        j = p % 64
        base = p - j
        if j < 32:
            sw[base + j + 32, p] = -1.0
        else:
            sw[base + j - 32, p] = 1.0
    ca = np.where(i[:, None] >= i[None, :], 0.0, -1e30).astype(ml_dtypes.bfloat16)
    ic_arr = np.zeros((128, 3), np.int32)
    ic_arr[:, 0] = 0x200000
    ic_arr[:, 1] = np.int64(0xffc00000) - (1 << 32)
    ic_arr[:, 2] = 0x7fffffff
    return {
        "ident": np.eye(128, dtype=ml_dtypes.bfloat16),
        "ones_col": np.ones((128, 1), np.float32),
        "ones_row_bf": np.ones((1, 128), dtype=ml_dtypes.bfloat16),
        "triu_incl_bf": (i[:, None] <= i[None, :]).astype(ml_dtypes.bfloat16),
        "fconsts": fc,
        "rope_swap": sw.astype(ml_dtypes.bfloat16),
        "causal_add": ca,
        "iconsts": ic_arr,
        "ident_f": np.eye(128, dtype=np.float32),
        "ones_col_bf": np.ones((128, 1), dtype=ml_dtypes.bfloat16),
    }


def prep_l0a(inputs):
    return {
        "l0a_wq": np.ascontiguousarray(inputs["L0_attn_a_Wq"].T),
        "l0a_wk": np.ascontiguousarray(inputs["L0_attn_a_Wk"].T),
        "l0a_wv": np.ascontiguousarray(inputs["L0_attn_a_Wv"].T),
        "l0a_wo": np.ascontiguousarray(inputs["L0_attn_a_Wo"].T),
        "l0a_bq": _colform(inputs["L0_attn_a_bq"]),
        "l0a_bk": _colform(inputs["L0_attn_a_bk"]),
        "l0a_bv": _colform(inputs["L0_attn_a_bv"]),
        "l0a_bo": _colform(inputs["L0_attn_a_bo"]),
    }


def rope_maps_l0b(S):
    half = 32
    pos = np.arange(S, dtype=np.float64)[:, None]
    inv = 1.0 / 10000.0 ** (np.arange(half, dtype=np.float64) / half)
    ang = pos * inv[None, :]
    cosv, sinv = np.cos(ang), np.sin(ang)
    p = np.arange(128)
    j = (p % 64) % 32
    cmap = cosv[:, j].T.astype(np.float32)
    smap = sinv[:, j].T.astype(np.float32)
    return np.ascontiguousarray(cmap), np.ascontiguousarray(smap)


def prep_l0b(inputs, S):
    cmap, smap = rope_maps_l0b(S)
    return {
        "l0b_wq": np.ascontiguousarray(inputs["L0_attn_b_W_q"]),
        "l0b_wdkv": np.ascontiguousarray(inputs["L0_attn_b_W_dkv"]),
        "l0b_wuk": np.ascontiguousarray(inputs["L0_attn_b_W_uk"]),
        "l0b_wuv": np.ascontiguousarray(inputs["L0_attn_b_W_uv"]),
        "l0b_wo": np.ascontiguousarray(inputs["L0_attn_b_W_o"]),
        "l0b_lng": _colform(inputs["L0_attn_b_ln_gamma"]),
        "l0b_lnb": _colform(inputs["L0_attn_b_ln_beta"]),
        "l0b_cmap": cmap,
        "l0b_smap": smap,
    }


def prep_l1a(inputs, S):
    perm = np.concatenate([np.arange(0, 64, 2), np.arange(1, 64, 2)])
    WQ, WK, WV = (np.asarray(inputs["L1_attn_a_" + n]) for n in ("WQ", "WK", "WV"))
    WQp = np.ascontiguousarray(WQ[:, perm, :].transpose(2, 0, 1).reshape(1024, 1024))
    WKp = np.ascontiguousarray(WK[:, perm, :].transpose(2, 0, 1).reshape(1024, 1024))
    WVp = np.ascontiguousarray(WV.transpose(2, 0, 1).reshape(1024, 1024))
    expo = np.arange(0, 64, 2, dtype=np.float64) / 64
    ang = 10.0 ** (-4.0 * expo)
    pos = np.arange(S, dtype=np.float64)[:, None]
    na = pos * ang[None, :]
    p = np.arange(128)
    j = (p % 64) % 32
    cmap = np.cos(na)[:, j].T.astype(np.float32)
    smap = np.sin(na)[:, j].T.astype(np.float32)
    import ml_dtypes
    C = 128
    lam = 1.0 - 2.0 ** (-5.0 - np.arange(16, dtype=np.float64))
    ii = np.arange(C)
    dmask = np.zeros((128, 16 * C), np.float64)
    lamq = np.zeros((128, 16 * C), np.float64)
    lamk = np.zeros((128, 16 * C), np.float64)
    for h in range(16):
        dm = np.where(ii[None, :] >= ii[:, None],
                      lam[h] ** np.maximum(ii[None, :] - ii[:, None], 0), 0.0)
        dmask[:, h * C:(h + 1) * C] = dm
        lamq[:, h * C:(h + 1) * C] = (lam[h] ** (ii + 1))[None, :]
        lamk[:, h * C:(h + 1) * C] = (lam[h] ** (C - 1 - ii))[None, :]
    selc = np.zeros((16, 8 * 128), np.float64)
    hs = np.zeros((128, 8 * 16), np.float64)
    for ci in range(8):
        for pp in range(128):
            g = 2 * ci + pp // 64
            selc[g, ci * 128 + pp] = 1.0
            hs[pp, ci * 16 + g] = 1.0
    return {
        "l1a_wq": WQp, "l1a_wk": WKp, "l1a_wv": WVp,
        "l1a_wo": np.ascontiguousarray(inputs["L1_attn_a_WO"]),
        "l1a_gamma": _colform(inputs["L1_attn_a_gamma"]),
        "l1a_beta": _colform(inputs["L1_attn_a_beta"]),
        "l1a_cmap": np.ascontiguousarray(cmap), "l1a_smap": np.ascontiguousarray(smap),
        "l1a_dmask": dmask.astype(ml_dtypes.bfloat16),
        "l1a_lamq": lamq.astype(ml_dtypes.bfloat16),
        "l1a_lamk": lamk.astype(ml_dtypes.bfloat16),
        "l1a_selc": selc.astype(ml_dtypes.bfloat16),
        "l1a_hs_bf": hs.astype(ml_dtypes.bfloat16),
        "l1a_hs_f": hs.astype(np.float32),
    }


def prep_l1b(inputs):
    import ml_dtypes
    wa = np.asarray(inputs["L1_attn_b_W_anchor"])
    wanc = np.zeros((128, 64), np.float32)
    wanc[0:64] = wa.T
    wanc[64:128] = wa.T
    return {
        "l1b_wemb": np.ascontiguousarray(np.asarray(inputs["L1_attn_b_W_emb"]).T),
        "l1b_wanc": wanc,
        "l1b_wout": np.ascontiguousarray(np.asarray(inputs["L1_attn_b_W_out"]).T),
        "l1b_bout": _colform(inputs["L1_attn_b_b_out"]),
    }


def prep_l1f(inputs):
    return {
        "l1f_wgate": np.ascontiguousarray(inputs["L1_ffn_w_gate"]),
        "l1f_wup": np.ascontiguousarray(inputs["L1_ffn_w_up"]),
        "l1f_wdown": np.ascontiguousarray(inputs["L1_ffn_w_down"]),
    }


def prep_l0f(inputs, S):
    import ml_dtypes
    wqkv = np.asarray(inputs["L0_ffn_wqkv"])
    wo = np.asarray(inputs["L0_ffn_wo"])
    w1 = np.asarray(inputs["L0_ffn_w1"])
    b1 = np.asarray(inputs["L0_ffn_b1"])
    w2 = np.asarray(inputs["L0_ffn_w2"])
    b2 = np.asarray(inputs["L0_ffn_b2"])
    HID, HIDP = 2730, 2816
    w1p = np.zeros((2, 1024, 2 * HIDP), np.float32)
    w1p[:, :, 0:HID] = w1[:, :, 0:HID]
    w1p[:, :, HIDP:HIDP + HID] = w1[:, :, HID:2 * HID]
    w1r = np.zeros_like(w1p)
    w1r[:, :, 0:1408] = w1p[:, :, 0:1408]
    w1r[:, :, 1408:2816] = w1p[:, :, HIDP:HIDP + 1408]
    w1r[:, :, 2816:4224] = w1p[:, :, 1408:2816]
    w1r[:, :, 4224:5632] = w1p[:, :, HIDP + 1408:2 * HIDP]
    b1p = np.zeros((2, 2 * HIDP), np.float32)
    b1p[:, 0:HID] = b1[:, 0:HID]
    b1p[:, HIDP:HIDP + HID] = b1[:, HID:2 * HID]
    b1r = np.zeros_like(b1p)
    b1r[:, 0:1408] = b1p[:, 0:1408]
    b1r[:, 1408:2816] = b1p[:, HIDP:HIDP + 1408]
    b1r[:, 2816:4224] = b1p[:, 1408:2816]
    b1r[:, 4224:5632] = b1p[:, HIDP + 1408:2 * HIDP]
    b1c = np.concatenate([_colform(b1r[l]) for l in range(2)], axis=1)
    w2p = np.zeros((2, HIDP, 1024), np.float32)
    w2p[:, 0:HID, :] = w2
    b2c = np.concatenate([_colform(b2[l]) for l in range(2)], axis=1)
    n = S
    DH, WIN = 64, 512
    half_idx = np.arange(0, DH, 2, dtype=np.float64)
    inv_freq = 1.0 / 10000.0 ** (half_idx / DH)
    tpos = np.arange(n, dtype=np.float64)
    freqs = tpos[:, None] * inv_freq[None, :]
    base = (half_idx + 0.4 * DH) / (1.4 * DH)
    power = (tpos - n // 2) / float(WIN // 2)
    scale = base[None, :] ** power[:, None]
    cos, sin = np.cos(freqs), np.sin(freqs)
    p = np.arange(128)
    j = (p % 64) % 32
    cq = (cos * scale)[:, j].T.astype(np.float32)
    sq = (sin * scale)[:, j].T.astype(np.float32)
    ck = (cos / scale)[:, j].T.astype(np.float32)
    sk = (sin / scale)[:, j].T.astype(np.float32)
    qsc = np.zeros((128, 2), np.float32)
    ksc = np.zeros((128, 2), np.float32)
    for l in range(2):
        qsc[:, l] = np.asarray(inputs["L0_ffn_q_scale"])[l][p % 64]
        ksc[:, l] = np.asarray(inputs["L0_ffn_k_scale"])[l][p % 64]
    lng = np.concatenate([_colform(np.asarray(inputs["L0_ffn_ln_g"])[l]) for l in range(2)], axis=1)
    lnb = np.concatenate([_colform(np.asarray(inputs["L0_ffn_ln_b"])[l]) for l in range(2)], axis=1)
    selc = np.zeros((16, 4 * 128), np.float64)
    hs = np.zeros((128, 4 * 16), np.float64)
    for ci in range(4):
        for pp in range(128):
            g = 2 * ci + pp // 64
            selc[g, ci * 128 + pp] = 1.0
            hs[pp, ci * 16 + g] = 1.0
    i = np.arange(128)
    win = np.where(i[:, None] <= i[None, :], 0.0, -1e30).astype(ml_dtypes.bfloat16)
    return {
        "l0f_wqkv": np.ascontiguousarray(wqkv),
        "l0f_wo": np.ascontiguousarray(wo),
        "l0f_w1": np.ascontiguousarray(w1r),
        "l0f_w2": np.ascontiguousarray(w2p),
        "l0f_b1": b1c, "l0f_b2": b2c,
        "l0f_qs": qsc, "l0f_ks": ksc,
        "l0f_lng": lng, "l0f_lnb": lnb,
        "l0f_cq": np.ascontiguousarray(cq), "l0f_sq": np.ascontiguousarray(sq),
        "l0f_ck": np.ascontiguousarray(ck), "l0f_sk": np.ascontiguousarray(sk),
        "l0f_hs_f": hs.astype(np.float32),
        "l0f_hs_bf": hs.astype(ml_dtypes.bfloat16),
        "l0f_selc": selc.astype(ml_dtypes.bfloat16),
        "win_add": win,
    }


def dram_specs_for(S, n_rows, weight_maps):
    specs = {"x": ((n_rows, D, S), F32), "y": ((n_rows, D, S), F32)}
    import ml_dtypes
    for nm, arr in weight_maps.items():
        if arr.dtype == ml_dtypes.bfloat16:
            dt = BF16
        elif arr.dtype == np.int32:
            dt = mybir.dt.int32
        else:
            dt = F32
        specs[nm] = (arr.shape, dt)
    return specs


SUBS_ALL = ["l0a", "l0f", "l0b", "l0f", "l1a", "l1f", "l1b", "l1f"]
FUSED = True


def all_preps(inputs, S):
    return {
        "l0a": prep_l0a(inputs),
        "l0b": prep_l0b(inputs, S),
        "l0f": prep_l0f(inputs, S),
        "l1a": prep_l1a(inputs, S),
        "l1b": prep_l1b(inputs),
        "l1f": prep_l1f(inputs),
    }


def kernel(**inputs):
    S, n_rows, n_cores = 2048, 2, 8
    consts = host_constants()
    preps = all_preps(inputs, S)
    x = np.asarray(inputs["L0_attn_a_x"])
    xs_core = [np.ascontiguousarray(
        x[c * n_rows:(c + 1) * n_rows].transpose(0, 2, 1)) for c in range(n_cores)]

    if FUSED:
        wmaps = dict(consts)
        for p in preps.values():
            wmaps.update(p)
        specs = dram_specs_for(S, n_rows, wmaps)
        nc = build_program(S, n_rows, SUBS_ALL, specs)
        in_maps = []
        for c in range(n_cores):
            mm = dict(wmaps)
            mm["x"] = xs_core[c]
            in_maps.append(mm)
        res = run_bass_kernel_spmd(nc, in_maps, core_ids=list(range(n_cores)))
        outs = [r["y"].transpose(0, 2, 1) for r in res.results]
        return np.ascontiguousarray(np.concatenate(outs, axis=0))

    ncs = {}
    for sub in SUBS_ALL:
        if sub in ncs:
            continue
        wmaps = dict(consts)
        wmaps.update(preps[sub])
        specs = dram_specs_for(S, n_rows, wmaps)
        ncs[sub] = (build_program(S, n_rows, [sub], specs), wmaps)
    for sub in SUBS_ALL:
        nc, wmaps = ncs[sub]
        in_maps = []
        for c in range(n_cores):
            mm = dict(wmaps)
            mm["x"] = xs_core[c]
            in_maps.append(mm)
        res = run_bass_kernel_spmd(nc, in_maps, core_ids=list(range(n_cores)))
        xs_core = [np.ascontiguousarray(r["y"]) for r in res.results]
    outs = [xc.transpose(0, 2, 1) for xc in xs_core]
    return np.ascontiguousarray(np.concatenate(outs, axis=0))
